# version 16
# baseline (speedup 1.0000x reference)
import numpy as np
import concourse.bass as bass
import concourse.mybir as mybir
from concourse.bass_utils import run_bass_kernel_spmd

F32 = mybir.dt.float32
BF16 = mybir.dt.bfloat16
AF = mybir.ActivationFunctionType
OP = mybir.AluOpType
AX = mybir.AxisListType

NCH = 8
NT = 8
TT = 512
D = 4096
CS = 3968

TWIST_SRC = {0: 4, 1: 5, 2: 6, 3: 7, 4: 0, 5: 1, 6: 2, 7: 3}
TWIST_OP = {0: OP.add, 1: OP.add, 2: OP.subtract, 3: OP.subtract,
            4: OP.subtract, 5: OP.subtract, 6: OP.add, 7: OP.add}

H4R = 0
SIGR = 2048
R_M = 0
R_Q = 512
R_SQ = 1024
S_GF = 4096
S_AR = 4097
S_K1 = 4098
S_K2 = 4099
S_T11 = 4100
ROWS_COLS = 4128
M_B4, M_GB3, M_EPS = range(3)
M_SCR = 3

BC_BANK = [7, 6, 0, 1]


def _alloc_tensors(nc):
    T = {}
    base = (nc._sbuf_addr_for_side("left") + 31) & ~31
    off = [base]

    def sb(name, shape, dt, at=None):
        pos = off[0] if at is None else (base + at)
        h = nc.alloc_sbuf_tensor_at(name, shape, dt, offset=pos)
        nbytes = int(np.prod(shape[1:])) * (2 if dt == BF16 else 4)
        if at is None:
            off[0] += (nbytes + 31) & ~31
        T[name] = h

    sb("OUT", [128, NCH * D], F32)
    sb("ROWS", [128, ROWS_COLS], F32)
    sb("MISC", [128, 16], F32)
    sb("GCOLS", [128, 64], F32)
    sb("B1", [128, 8], F32)
    sb("B2", [128, 4], F32)
    sb("B3", [128, 2], F32)
    sb("GB1", [128, 4], F32)
    sb("GB2", [128, 2], F32)
    sb("GMEANB", [128, 8], BF16)
    sb("XNCOL", [128, 8], BF16)
    sb("VB", [128, 8], BF16)
    sb("RSB", [128, 8], F32)
    sb("G1B", [128, 4], BF16)
    sb("G2B", [128, 2], BF16)
    sb("ONESC", [128, 1], BF16)
    sb("ONESCF", [128, 1], F32)
    sb("ONESB", [128, 128], F32)
    sb("TMP", [128, 2 * TT], F32)
    sb("CPS", [128, 4 * TT], F32)
    sb("XSQ", [128, 2 * 512], BF16)
    era = off[0] - base

    sb("W1", [128, 8 * 1024], BF16, at=era)
    sb("W2", [128, 8 * 512], BF16, at=era + 16384)
    sb("W3", [128, 4 * 256], BF16, at=era + 24576)
    sb("W4", [128, 2], BF16, at=era + 26624)
    sb("H1B", [128, 8 * TT], BF16, at=era + 26656)
    sb("H2B", [128, 4 * TT], BF16, at=era + 34848)
    sb("H3B", [128, 2 * TT], BF16, at=era + 38944)
    sb("OUTB", [128, 8 * TT], BF16, at=era + 40992)
    era2_end = era + 40992 + 8192
    sb("XT0", [128, NCH * TT], F32, at=era)
    sb("GT", [128, NCH * TT], F32, at=era + 16384)
    sb("XT1", [128, NCH * TT], F32, at=era + 32768)
    high = base + max(era2_end, era + 49152)
    assert high <= 229344, high

    T["pA"] = nc.alloc_psum_tensor("pA", [128, 2048], F32)
    T["pB"] = nc.alloc_psum_tensor("pB", [128, 2048], F32)
    return T


def _emit(me, e, nc, T, S, stage):
    C = {k: 0 for k in S}
    L = {}
    OUT, ROWS, MISC, GCOLS = T["OUT"], T["ROWS"], T["MISC"], T["GCOLS"]
    pA, pB = T["pA"], T["pB"]

    def inc(inst, sem, n=1):
        if inst is not None:
            inst.then_inc(S[sem], n)

    def note(sem, n=1):
        C[sem] += n

    def wait(sem, val=None):
        v = C[sem] if val is None else val
        if v > 0:
            e.wait_ge(S[sem], v)

    def scr(o, n=512):
        return ROWS[64:65, o:o + n]

    def cpr(o, n=512):
        return T["CPS"][64:65, o:o + n]

    def misc(col):
        return MISC[0:1, col:col + 1]

    mine = lambda name: me == name

    def bank_ap(b, rows=128, n=512):
        ps = pA if b < 4 else pB
        o = (b % 4) * 512
        return ps[0:rows, o:o + n]

    bank_free = {b: None for b in range(8)}

    def pe_claim(b):
        w = bank_free[b]
        if w is not None and mine("tensor"):
            wait(w[0], w[1])

    def bank_read_done(b, sem):
        bank_free[b] = (sem, C[sem])

    if mine("sync"):
        for j in range(NT):
            for c in range(NCH):
                inc(e.dma_start(out=OUT[:, c * D + j * TT: c * D + (j + 1) * TT],
                                in_=T["x_d"][c * 128:(c + 1) * 128,
                                             j * TT:(j + 1) * TT]),
                    "X%d" % j, 16)
        for nm, dst in (("b1r", T["B1"]), ("b2r", T["B2"]), ("b3r", T["B3"]),
                        ("gb1r", T["GB1"]), ("gb2r", T["GB2"])):
            inc(e.dma_start(out=dst[:, :], in_=T[nm + "_d"][:, :]), "B", 16)
        for nm, colo in (("b4r", M_B4), ("gb3r", M_GB3)):
            inc(e.dma_start(out=misc(colo), in_=T[nm + "_d"][:, :]), "B", 16)
        inc(e.dma_start(out=ROWS[64:65, S_AR:S_AR + 1],
                        in_=T["arr_d"][:, :]), "B", 16)
        inc(e.dma_start(out=T["y_d"][:, :], in_=T["x_d"][:, :]), "YC", 16)
    for j in range(NT):
        note("X%d" % j, 16 * NCH)
    note("B", 16 * 8)
    note("YC", 16)

    if mine("gpsimd"):
        for c in range(8):
            inc(e.dma_start(out=T["W1"][:, c * 1024:(c + 1) * 1024],
                            in_=T["w1_d"][c * 128:(c + 1) * 128, :]), "W2", 16)
        for c in range(8):
            inc(e.dma_start(out=T["W2"][:, c * 512:(c + 1) * 512],
                            in_=T["w2_d"][c * 128:(c + 1) * 128, :]), "W2", 16)
        for c in range(4):
            inc(e.dma_start(out=T["W3"][:, c * 256:(c + 1) * 256],
                            in_=T["w3_d"][c * 128:(c + 1) * 128, :]), "W2", 16)
        for c in range(2):
            inc(e.dma_start(out=T["W4"][:, c:c + 1],
                            in_=T["w4_d"][c * 128:(c + 1) * 128, :]), "W2", 16)
    note("W2", 16 * 22)
    L["cnw"] = C["W2"]

    if mine("vector"):
        e.memset(T["ONESC"][:, :], 1.0)
        e.memset(T["ONESCF"][:, :], 1.0)
        e.memset(MISC[0:1, M_EPS:M_EPS + 1], 1e-5)
        i2 = e.memset(T["ONESB"][:, :], 1.0)
        inc(i2, "VE")
    note("VE")
    L["ones"] = C["VE"]

    def p1_stageA_S(j):
        sb0 = 0 if j % 2 == 0 else 4
        pe_claim(sb0)
        if mine("tensor"):
            wait("X%d" % j, 16 * NCH)
            if j == 0:
                wait("VE", L["ones"])
            last = None
            for c in range(NCH):
                last = e.matmul(bank_ap(sb0, 1), T["ONESCF"][:, 0:1],
                                OUT[:, c * D + j * TT: c * D + (j + 1) * TT],
                                start=(c == 0), stop=(c == NCH - 1))
            inc(last, "PE")
        note("PE")
        L[("p1s", j)] = C["PE"]

        if mine("scalar"):
            wait("PE", L[("p1s", j)])
            if j >= 2:
                wait("VE", L[("p1rows", j // 2 - 1)])
            last = e.activation(out=cpr((j % 2) * 512), in_=bank_ap(sb0, 1),
                                func=AF.Copy, scale=1.0 / 1024.0)
            inc(last, "SC")
        note("SC")
        L[("p1evacm", j)] = C["SC"]
        bank_read_done(sb0, "SC")

    def p1_stageA_Q(j):
        sb0 = 0 if j % 2 == 0 else 4
        pe_claim(sb0 + 1)
        for c in range(NCH):
            sl = c % 2
            if mine("scalar"):
                if c == 0:
                    wait("X%d" % j, 16 * NCH)
                if j > 0 or c >= 2:
                    pg = (j, c - 2) if c >= 2 else (j - 1, 6 + c)
                    wait("PE", L[("p1q",) + pg])
                last = e.activation(
                    out=T["XSQ"][:, sl * 512:(sl + 1) * 512],
                    in_=OUT[:, c * D + j * TT: c * D + (j + 1) * TT],
                    func=AF.Square)
                inc(last, "SC")
            note("SC")
            L[("p1sq", j, c)] = C["SC"]

            if mine("tensor"):
                wait("SC", L[("p1sq", j, c)])
                last = e.matmul(bank_ap(sb0 + 1, 1), T["ONESC"][:, 0:1],
                                T["XSQ"][:, sl * 512:(sl + 1) * 512],
                                start=(c == 0), stop=(c == NCH - 1))
                inc(last, "PE")
            note("PE")
            L[("p1q", j, c)] = C["PE"]
        L[("p1mm", j, 1)] = C["PE"]

        if mine("scalar"):
            wait("PE", L[("p1mm", j, 1)])
            last = e.activation(out=cpr(1024 + (j % 2) * 512),
                                in_=bank_ap(sb0 + 1, 1),
                                func=AF.Copy, scale=1.0 / 1024.0)
            inc(last, "SC")
        note("SC")
        L[("p1evac", j)] = C["SC"]
        bank_read_done(sb0 + 1, "SC")

    def p1_rows(b):
        if mine("vector"):
            wait("SC", L[("p1evac", 2 * b + 1)])
            e.tensor_scalar(out=ROWS[32:33, b * 1024:(b + 1) * 1024],
                            in0=cpr(0, 1024), scalar1=-1.0, scalar2=None,
                            op0=OP.mult)
            C["CH"] += 1
            e.tensor_tensor(out=cpr(0, 1024), in0=cpr(0, 1024),
                            in1=cpr(0, 1024), op=OP.mult).then_inc(S["CH"], 1)
            e.wait_ge(S["CH"], C["CH"])
            last = e.tensor_tensor(out=cpr(1024, 1024), in0=cpr(1024, 1024),
                                   in1=cpr(0, 1024), op=OP.subtract)
            inc(last, "VE")
        else:
            C["CH"] += 1
        note("VE")
        L[("p1var", b)] = C["VE"]

        if mine("scalar"):
            wait("VE", L[("p1var", b)])
            last = e.activation(out=cpr(0, 1024), in_=cpr(1024, 1024),
                                func=AF.Sqrt, bias=misc(M_EPS))
            inc(last, "SC")
        note("SC")
        L[("p1sqrt", b)] = C["SC"]

        if mine("vector"):
            wait("SC", L[("p1sqrt", b)])
            last = e.reciprocal(out=ROWS[0:1, b * 1024:(b + 1) * 1024],
                                in_=cpr(0, 1024))
            inc(last, "VE")
        note("VE")
        L[("p1rows", b)] = C["VE"]

    def p1_norm(j):
        bb0 = 2 if j % 2 == 0 else 6
        pe_claim(bb0)
        pe_claim(bb0 + 1)
        if mine("tensor"):
            wait("VE", L[("p1rows", j // 2)])
            e.matmul(bank_ap(bb0), T["ONESB"][32:33, :],
                     ROWS[32:33, j * TT:(j + 1) * TT], start=True, stop=True)
            last = e.matmul(bank_ap(bb0 + 1), T["ONESB"][0:1, :],
                            ROWS[0:1, j * TT:(j + 1) * TT], start=True, stop=True)
            inc(last, "PE")
        note("PE")
        L[("p1bc", j)] = C["PE"]

        if mine("scalar"):
            wait("PE", L[("p1bc", j)])
            if j > 0:
                wait("VE", L[("p1norm", j - 1)])
            e.activation(out=T["TMP"][:, 0:TT], in_=bank_ap(bb0), func=AF.Copy)
            last = e.activation(out=T["TMP"][:, TT:2 * TT],
                                in_=bank_ap(bb0 + 1), func=AF.Copy)
            inc(last, "SC")
        note("SC")
        L[("p1bce", j)] = C["SC"]
        bank_read_done(bb0, "SC")
        bank_read_done(bb0 + 1, "SC")

        if mine("vector"):
            wait("SC", L[("p1bce", j)])
            for c in range(NCH):
                e.tensor_tensor(out=OUT[:, c * D + j * TT: c * D + (j + 1) * TT],
                                in0=OUT[:, c * D + j * TT: c * D + (j + 1) * TT],
                                in1=T["TMP"][:, 0:TT], op=OP.add)
            last = None
            for c in range(NCH):
                last = e.scalar_tensor_tensor(
                    out=OUT[:, c * D + j * TT: c * D + (j + 1) * TT],
                    in0=OUT[:, c * D + j * TT: c * D + (j + 1) * TT],
                    scalar=1.0, in1=T["TMP"][:, TT:2 * TT],
                    op0=OP.mult, op1=OP.mult,
                    accum_out=GCOLS[:, c * 8 + j: c * 8 + j + 1])
            if j == NT - 1:
                for c in range(NCH):
                    last = e.tensor_copy(out=T["XNCOL"][:, c:c + 1],
                                         in_=OUT[:, c * D + CS: c * D + CS + 1])
            inc(last, "VE")
        note("VE")
        L[("p1norm", j)] = C["VE"]

    n_cycles = {"xn": 0, "cycle1": 1, "cycle2": 2, "cycle3": 3, "full": 3}[stage]

    def half_before(cyc, half):
        return (cyc, 0) if half == 1 else ((cyc - 1, 1) if cyc > 0 else None)

    def emit_apply(pcyc, ptj):
        if mine("vector"):
            wait("SC", L[("cpevac", pcyc, ptj)])
            last = None
            for q in range(4):
                e.tensor_tensor(
                    out=T["TMP"][:, 0:TT],
                    in0=OUT[:, TWIST_SRC[q] * D + ptj * TT:
                            TWIST_SRC[q] * D + (ptj + 1) * TT],
                    in1=T["CPS"][:, (ptj % 4) * TT:(ptj % 4 + 1) * TT],
                    op=OP.mult)
                last = e.tensor_tensor(
                    out=T["TMP"][:, TT:2 * TT],
                    in0=OUT[:, q * D + ptj * TT: q * D + (ptj + 1) * TT],
                    in1=T["CPS"][:, (ptj % 4) * TT:(ptj % 4 + 1) * TT],
                    op=OP.mult)
                if q == 3:
                    inc(last, "VE")
                e.tensor_tensor(
                    out=OUT[:, q * D + ptj * TT: q * D + (ptj + 1) * TT],
                    in0=OUT[:, q * D + ptj * TT: q * D + (ptj + 1) * TT],
                    in1=T["TMP"][:, 0:TT], op=TWIST_OP[q])
                last = e.tensor_tensor(
                    out=OUT[:, (q + 4) * D + ptj * TT:
                            (q + 4) * D + (ptj + 1) * TT],
                    in0=OUT[:, (q + 4) * D + ptj * TT:
                            (q + 4) * D + (ptj + 1) * TT],
                    in1=T["TMP"][:, TT:2 * TT], op=TWIST_OP[q + 4])
            inc(last, "VE")
        note("VE", 2)
        L[("applym", pcyc, ptj)] = C["VE"] - 1
        L[("apply", pcyc, ptj)] = C["VE"]

    def emit_bcasts(pcyc, phalf):
        for i in range(4):
            ptj = phalf * 4 + i
            b = BC_BANK[i]
            pe_claim(b)
            if mine("tensor"):
                if i == 0:
                    wait("VE", L[("cprow", pcyc, phalf)])
                last = e.matmul(bank_ap(b), T["ONESB"][64:65, :],
                                scr(SIGR + i * 512), start=True, stop=True)
                inc(last, "PE")
            note("PE")
            L[("bc", pcyc, ptj)] = C["PE"]

    def emit_cpevac(pcyc, ptj, i, pp):
        b = BC_BANK[i]
        if mine("scalar"):
            wait("PE", L[("bc", pcyc, ptj)])
            if pp is not None:
                wait("VE", L[("applym", pp[0], pp[1] + i)])
            last = e.activation(out=T["CPS"][:, i * TT:(i + 1) * TT],
                                in_=bank_ap(b), func=AF.Copy)
            inc(last, "SC")
        note("SC")
        L[("cpevac", pcyc, ptj)] = C["SC"]
        bank_read_done(b, "SC")

    def emit_cyc_tile(cyc, half, i, do_apply=True, do_bc=True):
        tj = half * 4 + i
        prev = half_before(cyc, half)
        pp = half_before(*prev) if prev is not None else None

        if mine("vector"):
            pg = tj - 1 if tj > 0 else (NT - 1 if cyc > 0 else None)
            if pg is not None:
                pc = cyc if tj > 0 else cyc - 1
                wait("PE", L[("h1mm", pc, pg, 7)])
            last = None
            for c in range(NCH):
                last = e.tensor_copy(
                    out=T["OUTB"][:, c * TT:(c + 1) * TT],
                    in_=OUT[:, c * D + tj * TT: c * D + (tj + 1) * TT])
            inc(last, "VE")
        note("VE")
        L[("cast", cyc, tj)] = C["VE"]

        if prev is not None and i >= 1 and do_apply:
            emit_apply(prev[0], prev[1] * 4 + (i - 1))

        for mc in range(8):
            pe_claim(mc)
            if mine("tensor"):
                if mc == 0:
                    wait("VE", L[("cast", cyc, tj)])
                    if cyc == 0 and tj == 0:
                        wait("W2", L["cnw"])
                last = None
                for kc in range(8):
                    last = e.matmul(
                        bank_ap(mc),
                        T["W1"][:, kc * 1024 + mc * 128: kc * 1024 + (mc + 1) * 128],
                        T["OUTB"][:, kc * TT:(kc + 1) * TT],
                        start=(kc == 0), stop=(kc == 7))
                inc(last, "PE")
            note("PE")
            L[("h1mm", cyc, tj, mc)] = C["PE"]

        for mc in range(8):
            if mine("scalar"):
                if cyc == 0 and tj == 0 and mc == 0:
                    wait("B")
                wait("PE", L[("h1mm", cyc, tj, mc)])
                last = e.activation(out=T["H1B"][:, mc * TT:(mc + 1) * TT],
                                    in_=bank_ap(mc), func=AF.Gelu,
                                    bias=T["B1"][:, mc:mc + 1])
                inc(last, "SC")
            note("SC")
            L[("h1evac", cyc, tj, mc)] = C["SC"]
            bank_read_done(mc, "SC")

        for mc in range(4):
            pe_claim(mc)
            if mine("tensor"):
                if mc == 0:
                    wait("SC", L[("h1evac", cyc, tj, 7)])
                last = None
                for kc in range(8):
                    last = e.matmul(
                        bank_ap(mc),
                        T["W2"][:, kc * 512 + mc * 128: kc * 512 + (mc + 1) * 128],
                        T["H1B"][:, kc * TT:(kc + 1) * TT],
                        start=(kc == 0), stop=(kc == 7))
                inc(last, "PE")
            note("PE")
            L[("h2mm", cyc, tj, mc)] = C["PE"]

        for mc in range(4):
            if mine("scalar"):
                wait("PE", L[("h2mm", cyc, tj, mc)])
                last = e.activation(out=T["H2B"][:, mc * TT:(mc + 1) * TT],
                                    in_=bank_ap(mc), func=AF.Gelu,
                                    bias=T["B2"][:, mc:mc + 1])
                inc(last, "SC")
            note("SC")
            L[("h2evac", cyc, tj, mc)] = C["SC"]
            bank_read_done(mc, "SC")

        for mc in range(2):
            pe_claim(4 + mc)
            if mine("tensor"):
                if mc == 0:
                    wait("SC", L[("h2evac", cyc, tj, 3)])
                last = None
                for kc in range(4):
                    last = e.matmul(
                        bank_ap(4 + mc),
                        T["W3"][:, kc * 256 + mc * 128: kc * 256 + (mc + 1) * 128],
                        T["H2B"][:, kc * TT:(kc + 1) * TT],
                        start=(kc == 0), stop=(kc == 3))
                inc(last, "PE")
            note("PE")
            L[("h3mm", cyc, tj, mc)] = C["PE"]

        for mc in range(2):
            if mine("scalar"):
                wait("PE", L[("h3mm", cyc, tj, mc)])
                last = e.activation(out=T["H3B"][:, mc * TT:(mc + 1) * TT],
                                    in_=bank_ap(4 + mc), func=AF.Gelu,
                                    bias=T["B3"][:, mc:mc + 1])
                inc(last, "SC")
            note("SC")
            L[("h3evac", cyc, tj, mc)] = C["SC"]
            bank_read_done(4 + mc, "SC")

        hb = 6 + (i % 2)
        pe_claim(hb)
        if mine("tensor"):
            wait("SC", L[("h3evac", cyc, tj, 1)])
            last = None
            for kc in range(2):
                last = e.matmul(bank_ap(hb, 1), T["W4"][:, kc:kc + 1],
                                T["H3B"][:, kc * TT:(kc + 1) * TT],
                                start=(kc == 0), stop=(kc == 1))
            inc(last, "PE")
        note("PE")
        L[("h4", cyc, tj)] = C["PE"]

        if mine("scalar"):
            wait("PE", L[("h4", cyc, tj)])
            last = e.activation(out=scr(H4R + i * 512),
                                in_=bank_ap(hb, 1), func=AF.Copy)
            inc(last, "SC")
        note("SC")
        L[("h4evac", cyc, tj)] = C["SC"]
        bank_read_done(hb, "SC")

        if i == 0 and prev is not None and do_bc:
            emit_bcasts(*prev)
            for k in range(4):
                emit_cpevac(prev[0], prev[1] * 4 + k, k,
                            (pp[0], pp[1] * 4) if pp is not None else None)
        if slab_queue:
            slab_queue.pop(0)()

    def emit_sig(cyc, half):
        prev = half_before(cyc, half)
        if mine("scalar"):
            if prev is not None:
                wait("VE", L[("cprow",) + prev])
                wait("PE", L[("bc", prev[0], prev[1] * 4 + 3)])
            last = e.activation(out=scr(SIGR, 2048), in_=scr(H4R, 2048),
                                func=AF.Sigmoid, bias=misc(M_B4))
            inc(last, "SC")
        note("SC")
        L[("sig", cyc, half)] = C["SC"]

    def emit_cprow(cyc, half):
        if mine("vector"):
            wait("SC", L[("sig", cyc, half)])
            last = e.tensor_scalar(out=scr(SIGR, 2048), in0=scr(SIGR, 2048),
                                   scalar1=scr(S_K2, 1), scalar2=scr(S_K1, 1),
                                   op0=OP.mult, op1=OP.add)
            inc(last, "VE")
        note("VE")
        L[("cprow", cyc, half)] = C["VE"]

    def emit_half(cyc, half):
        prev = half_before(cyc, half)
        for i in range(4):
            emit_cyc_tile(cyc, half, i)
        if prev is not None:
            emit_apply(prev[0], prev[1] * 4 + 3)
        emit_sig(cyc, half)
        emit_cprow(cyc, half)

    slab_state = {"idx": 0, "prev_mm": None}
    slab_queue = []

    def slab_group(src_d, kc_n, mc, w, rhs_fn, bank, evac_fn):
        gidx = slab_state["idx"]
        slab_state["idx"] += 1
        if mine("gpsimd"):
            if gidx == 0:
                wait("PE", L[("p1q", NT - 1, NCH - 1)])
            else:
                wait("PE", slab_state["prev_mm"])
            for kc in range(kc_n):
                inc(e.dma_start(
                    out=T["XSQ"][:, kc * w:(kc + 1) * w],
                    in_=src_d[kc * 128:(kc + 1) * 128, mc * w:(mc + 1) * w]),
                    "WD", 16)
        note("WD", 16 * kc_n)
        rows = 128 if w == 128 else 1
        pe_claim(bank)
        if mine("tensor"):
            wait("WD")
            last = None
            for kc in range(kc_n):
                last = e.matmul(bank_ap(bank, rows, 1),
                                T["XSQ"][:, kc * w:(kc + 1) * w],
                                rhs_fn(kc), start=(kc == 0), stop=(kc == kc_n - 1))
            inc(last, "PE")
        note("PE")
        L[("slab", gidx)] = C["PE"]
        slab_state["prev_mm"] = L[("slab", gidx)]
        evac_fn(bank, rows)

    def _evac(out_ap, func, bias):
        def f(bank, rows):
            if mine("scalar"):
                wait("PE", slab_state["prev_mm"])
                last = e.activation(out=out_ap, in_=bank_ap(bank, rows, 1),
                                    func=func, bias=bias)
                inc(last, "SC")
            note("SC")
            L[("slabe", slab_state["idx"] - 1)] = C["SC"]
            bank_read_done(bank, "SC")
        return f

    if stage == "xn":
        for j in range(NT):
            if j >= 2 and j % 2 == 0:
                p1_rows(j // 2 - 1)
            p1_stageA(j)
            if j >= 2:
                p1_norm(j - 2)
        p1_rows(3)
        p1_norm(NT - 2)
        p1_norm(NT - 1)
        _dump_out_and_finish(me, e, T, S, C, L)
        return

    for j in range(NT):
        if j >= 2 and j % 2 == 0:
            p1_rows(j // 2 - 1)
        p1_stageA_S(j)
        if j >= 2:
            p1_norm(j - 2)
            if j - 2 < 4:
                emit_cyc_tile(0, 0, j - 2)
            else:
                emit_cyc_tile(0, 1, j - 6, do_apply=False, do_bc=False)
        p1_stageA_Q(j)
        if j >= 2 and j - 2 == 3:
            emit_sig(0, 0)
    p1_rows(3)
    p1_norm(NT - 2)
    p1_norm(NT - 1)

    if mine("vector"):
        last = None
        for c in range(NCH):
            last = e.tensor_reduce(out=MISC[:, M_SCR + c:M_SCR + c + 1],
                                   in_=GCOLS[:, c * 8:(c + 1) * 8],
                                   axis=AX.X, op=OP.add)
        C["CH"] += 1
        last.then_inc(S["CH"], 1)
        e.wait_ge(S["CH"], C["CH"])
        for c in range(NCH):
            last = e.tensor_scalar(out=T["GMEANB"][:, c:c + 1],
                                   in0=MISC[:, M_SCR + c:M_SCR + c + 1],
                                   scalar1=1.0 / 4096.0, scalar2=None, op0=OP.mult)
        inc(last, "VE")
    else:
        C["CH"] += 1
    note("VE")
    L["gmean"] = C["VE"]

    def g1_rhs(kc):
        return T["GMEANB"][:, kc:kc + 1]
    for mc in range(4):
        if mc == 0 and mine("tensor"):
            wait("VE", L["gmean"])
        slab_group(T["g1_d"], 8, mc, 128, g1_rhs, 2 + (mc % 2),
                   _evac(T["G1B"][:, mc:mc + 1], AF.Gelu, T["GB1"][:, mc:mc + 1]))
    def g2_rhs(kc):
        return T["G1B"][:, kc:kc + 1]
    for mc in range(2):
        slab_group(T["g2_d"], 4, mc, 128, g2_rhs, 2 + (mc % 2),
                   _evac(T["G2B"][:, mc:mc + 1], AF.Gelu, T["GB2"][:, mc:mc + 1]))
    def g3_rhs(kc):
        return T["G2B"][:, kc:kc + 1]
    slab_group(T["g3_d"], 2, 0, 1, g3_rhs, 2,
               _evac(scr(S_GF, 1), AF.Sigmoid, misc(M_GB3)))

    if mine("vector"):
        wait("SC", L[("slabe", slab_state["idx"] - 1)])
        def _step(inst):
            C["CH"] += 1
            inst.then_inc(S["CH"], 1)
            e.wait_ge(S["CH"], C["CH"])
        _step(e.tensor_scalar(out=scr(S_K2, 1), in0=scr(S_AR, 1), scalar1=0.6,
                              scalar2=None, op0=OP.mult))
        _step(e.tensor_scalar(out=scr(S_T11, 1), in0=scr(S_GF, 1),
                              scalar1=scr(S_AR, 1), scalar2=None, op0=OP.mult))
        _step(e.tensor_scalar(out=scr(S_T11, 1), in0=scr(S_T11, 1), scalar1=1.4,
                              scalar2=0.1, op0=OP.mult, op1=OP.add))
        _step(e.tensor_tensor(out=scr(S_K1, 1), in0=scr(S_T11, 1),
                              in1=scr(S_AR, 1), op=OP.subtract))
        inc(e.engine_nop(), "VE")
    else:
        C["CH"] += 4
    note("VE")
    L["kconst"] = C["VE"]

    emit_cprow(0, 0)

    def v_rhs(kc):
        return T["XNCOL"][:, kc:kc + 1]

    def r_rhs(kc):
        return T["VB"][:, kc:kc + 1]

    def make_vgroup(mc):
        def f():
            if mc == 0 and mine("tensor"):
                wait("VE", L[("p1norm", NT - 1)])
            slab_group(T["wvt_d"], 8, mc, 128, v_rhs, 2 + (mc % 2),
                       _evac(T["VB"][:, mc:mc + 1], AF.Copy, 0.0))
        return f

    def make_rgroup(mc):
        def f():
            slab_group(T["wot_d"], 8, mc, 128, r_rhs, 2 + (mc % 2),
                       _evac(T["RSB"][:, mc:mc + 1], AF.Copy, 0.0))
        return f

    for mc in range(8):
        slab_queue.append(make_vgroup(mc))
    for mc in range(8):
        slab_queue.append(make_rgroup(mc))

    emit_cyc_tile(0, 1, 2, do_apply=False, do_bc=False)
    emit_bcasts(0, 0)
    for k in range(4):
        emit_cpevac(0, k, k, None)
    emit_apply(0, 0)
    emit_apply(0, 1)
    emit_cyc_tile(0, 1, 3, do_apply=False)
    emit_apply(0, 2)
    emit_apply(0, 3)
    emit_sig(0, 1)
    emit_cprow(0, 1)

    for cyc in range(1, n_cycles):
        emit_half(cyc, 0)
        emit_half(cyc, 1)
    while slab_queue:
        slab_queue.pop(0)()

    lastc = n_cycles - 1
    emit_bcasts(lastc, 1)
    for i in range(4):
        emit_cpevac(lastc, 4 + i, i, (lastc, 0))
    if stage in ("cycle1", "cycle2", "cycle3"):
        for i in range(4):
            emit_apply(lastc, 4 + i)

    if stage in ("cycle1", "cycle2", "cycle3"):
        _dump_out_and_finish(me, e, T, S, C, L,
                             after=("apply", lastc, NT - 1))
        return

    for j in range(NT):
        xt = T["XT0"] if j % 2 == 0 else T["XT1"]
        bnm = 2 if j % 2 == 0 else 6
        brs = bnm + 1
        if j < 4:
            emit_apply(n_cycles - 1, 4 + j)
        if mine("sync"):
            if j <= 1:
                wait("PE", L[("bc", n_cycles - 1, NT - 1)])
                wait("SC", L[("h4evac", n_cycles - 1, NT - 1)])
            else:
                wait("VE", L[("p3dve", j - 2)])
            xsrc = T["x_d"].rearrange("(c p) t -> p c t", p=128)
            inc(e.dma_start(out=xt[:, :].rearrange("p (c t) -> p c t", c=NCH),
                            in_=xsrc[:, :, j * TT:(j + 1) * TT]),
                "XTA" if j % 2 == 0 else "XTB", 16)
        xsem = "XTA" if j % 2 == 0 else "XTB"
        note(xsem, 16)
        L[("xt", j)] = (xsem, C[xsem])

        pe_claim(bnm)
        pe_claim(brs)
        if mine("tensor"):
            wait("VE", L[("apply", n_cycles - 1, j)])
            e.matmul(bank_ap(bnm), T["ONESB"][32:33, :],
                     ROWS[32:33, j * TT:(j + 1) * TT], start=True, stop=True)
            last = e.matmul(bank_ap(brs), T["ONESB"][0:1, :],
                            ROWS[0:1, j * TT:(j + 1) * TT], start=True, stop=True)
            inc(last, "PE")
        note("PE")
        L[("p3w", j)] = C["PE"]

        for q in range(4):
            cq = range(2 * q, 2 * q + 2)
            if mine("scalar"):
                if j == 0 and q == 0:
                    wait("PE", L[("bc", n_cycles - 1, NT - 1)])
                    wait("VE", L[("apply", n_cycles - 1, 0)])
                else:
                    if q == 0:
                        wait("VE", L[("apply", n_cycles - 1, j)])
                    pjq = (j, q - 2) if q >= 2 else (j - 1, q + 2)
                    if pjq[0] >= 0:
                        wait("ST", L.get(("p3stq",) + pjq, 0))
                last = None
                for c in cq:
                    last = e.activation(out=T["GT"][:, c * TT:(c + 1) * TT],
                                        in_=OUT[:, c * D + j * TT: c * D + (j + 1) * TT],
                                        func=AF.Silu)
                inc(last, "SC")
            note("SC")
            L[("p3silu", j, q)] = C["SC"]

            if mine("vector"):
                if q == 0:
                    wait(*L[("xt", j)])
                    wait("PE", L[("p3w", j)])
                wait("SC", L[("p3silu", j, q)])
                last = None
                W = 384 if j == NT - 1 else TT
                for c in cq:
                    e.tensor_tensor(out=T["TMP"][:, 0:W],
                                    in0=xt[:, c * TT:c * TT + W],
                                    in1=bank_ap(bnm, 128, W), op=OP.add)
                    last = e.tensor_tensor(out=T["TMP"][:, 0:W],
                                           in0=T["TMP"][:, 0:W],
                                           in1=bank_ap(brs, 128, W), op=OP.mult)
                    if c == 2 * q + 1 and q == 3:
                        inc(last, "VE")
                    if j == NT - 1:
                        e.tensor_scalar(
                            out=T["GT"][:, c * TT + 384: c * TT + 512],
                            in0=T["GT"][:, c * TT + 384: c * TT + 512],
                            scalar1=T["RSB"][:, c:c + 1], scalar2=None,
                            op0=OP.mult)
                    last = e.tensor_tensor(out=T["GT"][:, c * TT:c * TT + W],
                                           in0=T["GT"][:, c * TT:c * TT + W],
                                           in1=T["TMP"][:, 0:W], op=OP.mult)
                inc(last, "VE")
            note("VE", 2 if q == 3 else 1)
            L[("p3q", j, q)] = C["VE"]

            if mine("gpsimd"):
                if j == 0 and q == 0:
                    wait("YC")
                wait("VE", L[("p3q", j, q)])
                ydst = T["y_d"].rearrange("(c p) t -> p c t", p=128)
                inc(e.dma_start(
                    out=ydst[:, 2 * q:2 * q + 2, j * TT:(j + 1) * TT],
                    in_=T["GT"][:, q * 1024:(q + 1) * 1024].rearrange(
                        "p (c t) -> p c t", c=2),
                    accum_op=OP.add), "ST", 16)
            note("ST", 16)
            L[("p3stq", j, q)] = C["ST"]
        L[("p3dve", j)] = C["VE"]
        bank_read_done(bnm, "VE")
        bank_read_done(brs, "VE")

    if mine("sync"):
        wait("ST")
        wait("YC")


def _dump_out_and_finish(me, e, T, S, C, L, after=None):
    C["ST"] += 16 * NCH
    if me == "sync":
        if after is None:
            e.wait_ge(S["VE"], L[("p1norm", NT - 1)])
        else:
            e.wait_ge(S["VE"], L[after])
        for c in range(NCH):
            e.dma_start(out=T["y_d"][c * 128:(c + 1) * 128, :],
                        in_=T["OUT"][:, c * D:(c + 1) * D]).then_inc(S["ST"], 16)
        e.wait_ge(S["ST"], C["ST"])


def build_program(stage="full"):
    nc = bass.Bass("TRN2", target_bir_lowering=False, debug=False)
    T = {}
    T["x_d"] = nc.dram_tensor("x", [1024, 4096], F32, kind="ExternalInput").ap()
    for nm, shape in (("w1", [1024, 1024]), ("w2", [1024, 512]),
                      ("w3", [512, 256]), ("w4", [256, 1]),
                      ("g1", [1024, 512]), ("g2", [512, 256]), ("g3", [256, 1]),
                      ("wvt", [1024, 1024]), ("wot", [1024, 1024]),
                      ("b1r", [128, 8]), ("b2r", [128, 4]), ("b3r", [128, 2]),
                      ("gb1r", [128, 4]), ("gb2r", [128, 2]),
                      ("b4r", [1, 1]), ("gb3r", [1, 1]), ("arr", [1, 1])):
        T[nm + "_d"] = nc.dram_tensor(nm, shape, F32, kind="ExternalInput").ap()
    T["y_d"] = nc.dram_tensor("y", [1024, 4096], F32, kind="ExternalOutput").ap()

    T.update(_alloc_tensors(nc))

    from contextlib import ExitStack
    with ExitStack() as stack:
        S = {}
        for nm in ("X", "B", "WD", "W2", "PE", "SC", "VE", "ST", "XT", "CH",
                   "X0", "X1", "X2", "X3", "X4", "X5", "X6", "X7",
                   "XTA", "XTB", "YC"):
            S[nm] = stack.enter_context(nc.semaphore(nm))
        with nc.Block() as block:
            block.sync(lambda e: _emit("sync", e, nc, T, S, stage))
            block.gpsimd(lambda e: _emit("gpsimd", e, nc, T, S, stage))
            block.tensor(lambda e: _emit("tensor", e, nc, T, S, stage))
            block.scalar(lambda e: _emit("scalar", e, nc, T, S, stage))
            block.vector(lambda e: _emit("vector", e, nc, T, S, stage))
    return nc


def make_in_maps(inputs):
    x = inputs["L0_attn_inputs"]
    c = np.ascontiguousarray
    shared = {
        "w1": c(inputs["L0_ffn_cn_w1"]), "w2": c(inputs["L0_ffn_cn_w2"]),
        "w3": c(inputs["L0_ffn_cn_w3"]), "w4": c(inputs["L0_ffn_cn_w4"]),
        "g1": c(inputs["L0_ffn_gc_w1"]), "g2": c(inputs["L0_ffn_gc_w2"]),
        "g3": c(inputs["L0_ffn_gc_w3"]),
        "wvt": c(inputs["L0_attn_Wv"].T), "wot": c(inputs["L0_attn_Wo"].T),
        "b1r": c(inputs["L0_ffn_cn_b1"].reshape(8, 128).T),
        "b2r": c(inputs["L0_ffn_cn_b2"].reshape(4, 128).T),
        "b3r": c(inputs["L0_ffn_cn_b3"].reshape(2, 128).T),
        "gb1r": c(inputs["L0_ffn_gc_b1"].reshape(4, 128).T),
        "gb2r": c(inputs["L0_ffn_gc_b2"].reshape(2, 128).T),
        "b4r": c(inputs["L0_ffn_cn_b4"].reshape(1, 1)),
        "gb3r": c(inputs["L0_ffn_gc_b3"].reshape(1, 1)),
        "arr": c(inputs["L0_ffn_adaptive_range"].reshape(1, 1)),
    }
    in_maps = []
    for b in range(8):
        m = dict(shared)
        m["x"] = c(x[b].T)
        in_maps.append(m)
    return in_maps


def kernel(**inputs):
    nc = build_program("full")
    in_maps = make_in_maps(inputs)
    res = run_bass_kernel_spmd(nc, in_maps, core_ids=list(range(8)))
    out = np.empty((8, 4096, 1024), dtype=np.float32)
    for b in range(8):
        out[b] = res.results[b]["y"].T
    return out


# version 17
# speedup vs baseline: 1.0493x; 1.0493x over previous
import numpy as np
import concourse.bass as bass
import concourse.mybir as mybir
from concourse.bass_utils import run_bass_kernel_spmd

F32 = mybir.dt.float32
BF16 = mybir.dt.bfloat16
AF = mybir.ActivationFunctionType
OP = mybir.AluOpType
AX = mybir.AxisListType

NCH = 8
NT = 8
TT = 512
D = 4096
CS = 3968

TWIST_SRC = {0: 4, 1: 5, 2: 6, 3: 7, 4: 0, 5: 1, 6: 2, 7: 3}
TWIST_OP = {0: OP.add, 1: OP.add, 2: OP.subtract, 3: OP.subtract,
            4: OP.subtract, 5: OP.subtract, 6: OP.add, 7: OP.add}

H4R = 0
SIGR = 2048
R_M = 0
R_Q = 512
R_SQ = 1024
S_GF = 4096
S_AR = 4097
S_K1 = 4098
S_K2 = 4099
S_T11 = 4100
ROWS_COLS = 4128
M_B4, M_GB3, M_EPS = range(3)
M_SCR = 3

BC_BANK = [7, 6, 0, 1]


def _alloc_tensors(nc):
    T = {}
    base = (nc._sbuf_addr_for_side("left") + 31) & ~31
    off = [base]

    def sb(name, shape, dt, at=None):
        pos = off[0] if at is None else (base + at)
        h = nc.alloc_sbuf_tensor_at(name, shape, dt, offset=pos)
        nbytes = int(np.prod(shape[1:])) * (2 if dt == BF16 else 4)
        if at is None:
            off[0] += (nbytes + 31) & ~31
        T[name] = h

    sb("OUT", [128, NCH * D], F32)
    sb("ROWS", [128, ROWS_COLS], F32)
    sb("MISC", [128, 16], F32)
    sb("GCOLS", [128, 64], F32)
    sb("B1", [128, 8], F32)
    sb("B2", [128, 4], F32)
    sb("B3", [128, 2], F32)
    sb("GB1", [128, 4], F32)
    sb("GB2", [128, 2], F32)
    sb("GMEANB", [128, 8], BF16)
    sb("XNCOL", [128, 8], BF16)
    sb("VB", [128, 8], BF16)
    sb("RSB", [128, 8], F32)
    sb("G1B", [128, 4], BF16)
    sb("G2B", [128, 2], BF16)
    sb("ONESC", [128, 1], BF16)
    sb("ONESB", [128, 128], F32)
    sb("EYE", [128, 128], F32)
    sb("TMP", [128, 2 * TT], F32)
    sb("CPS", [128, 4 * TT], F32)
    era = off[0] - base

    sb("XB", [128, 4 * TT], BF16, at=era)
    sb("XSQ", [128, 4 * TT], BF16, at=era + 4096)
    sb("XB2", [128, 4 * TT], BF16, at=era + 8192)
    sb("XSQ2", [128, 4 * TT], BF16, at=era + 12288)
    sb("GC1", [128, 8 * 512], BF16, at=era)
    sb("GC2", [128, 4 * 256], BF16, at=era + 8192)
    sb("GC3", [128, 2], BF16, at=era + 10240)
    sb("WVT", [128, 8 * 1024], BF16, at=era + 16384)
    sb("WOT", [128, 8 * 1024], BF16, at=era)
    sb("W1", [128, 8 * 1024], BF16, at=era)
    sb("W2", [128, 8 * 512], BF16, at=era + 16384)
    sb("W3", [128, 4 * 256], BF16, at=era + 24576)
    sb("W4", [128, 2], BF16, at=era + 26624)
    sb("H1B", [128, 8 * TT], BF16, at=era + 26656)
    sb("H2B", [128, 4 * TT], BF16, at=era + 34848)
    sb("H3B", [128, 2 * TT], BF16, at=era + 38944)
    sb("OUTB", [128, 8 * TT], BF16, at=era + 40992)
    era2_end = era + 40992 + 8192
    sb("XT0", [128, NCH * TT], F32, at=era)
    sb("GT", [128, NCH * TT], F32, at=era + 16384)
    sb("XT1", [128, NCH * TT], F32, at=era + 32768)
    high = base + max(era2_end, era + 49152)
    assert high <= 229344, high

    T["pA"] = nc.alloc_psum_tensor("pA", [128, 2048], F32)
    T["pB"] = nc.alloc_psum_tensor("pB", [128, 2048], F32)
    return T


def _emit(me, e, nc, T, S, stage):
    C = {k: 0 for k in S}
    L = {}
    OUT, ROWS, MISC, GCOLS = T["OUT"], T["ROWS"], T["MISC"], T["GCOLS"]
    pA, pB = T["pA"], T["pB"]

    def inc(inst, sem, n=1):
        if inst is not None:
            inst.then_inc(S[sem], n)

    def note(sem, n=1):
        C[sem] += n

    def wait(sem, val=None):
        v = C[sem] if val is None else val
        if v > 0:
            e.wait_ge(S[sem], v)

    def scr(o, n=512):
        return ROWS[64:65, o:o + n]

    def misc(col):
        return MISC[0:1, col:col + 1]

    mine = lambda name: me == name

    def bank_ap(b, rows=128, n=512):
        ps = pA if b < 4 else pB
        o = (b % 4) * 512
        return ps[0:rows, o:o + n]

    bank_free = {b: None for b in range(8)}

    def pe_claim(b):
        w = bank_free[b]
        if w is not None and mine("tensor"):
            wait(w[0], w[1])

    def bank_read_done(b, sem):
        bank_free[b] = (sem, C[sem])

    if mine("sync"):
        for j in range(NT):
            for c in range(NCH):
                inc(e.dma_start(out=OUT[:, c * D + j * TT: c * D + (j + 1) * TT],
                                in_=T["x_d"][c * 128:(c + 1) * 128,
                                             j * TT:(j + 1) * TT]),
                    "X%d" % j, 16)
        for nm, dst in (("b1r", T["B1"]), ("b2r", T["B2"]), ("b3r", T["B3"]),
                        ("gb1r", T["GB1"]), ("gb2r", T["GB2"])):
            inc(e.dma_start(out=dst[:, :], in_=T[nm + "_d"][:, :]), "B", 16)
        for nm, colo in (("b4r", M_B4), ("gb3r", M_GB3)):
            inc(e.dma_start(out=misc(colo), in_=T[nm + "_d"][:, :]), "B", 16)
        inc(e.dma_start(out=ROWS[64:65, S_AR:S_AR + 1],
                        in_=T["arr_d"][:, :]), "B", 16)
        inc(e.dma_start(out=T["EYE"][:, :], in_=T["eye_d"][:, :]), "B", 16)
    for j in range(NT):
        note("X%d" % j, 16 * NCH)
    note("B", 16 * 9)

    if mine("gpsimd"):
        for c in range(8):
            inc(e.dma_start(out=T["WVT"][:, c * 1024:(c + 1) * 1024],
                            in_=T["wvt_d"][c * 128:(c + 1) * 128, :]), "WD", 16)
    note("WD", 16 * 8)
    L["wvload"] = C["WD"]

    if mine("vector"):
        e.memset(T["ONESC"][:, :], 1.0)
        e.memset(MISC[0:1, M_EPS:M_EPS + 1], 1e-5)
        i2 = e.memset(T["ONESB"][:, :], 1.0)
        inc(i2, "VE")
    note("VE")
    L["ones"] = C["VE"]

    def p1_stageA(j):
        sb0 = 0 if j % 2 == 0 else 4
        xb = T["XB"] if j % 2 == 0 else T["XB2"]
        xsq = T["XSQ"] if j % 2 == 0 else T["XSQ2"]
        for h in range(2):
            cs4 = range(4 * h, 4 * h + 4)
            if mine("vector"):
                if h == 1:
                    wait("PE", L[("p1mm", j, 0)])
                elif j >= 2:
                    wait("PE", L[("p1mm", j - 2, 1)])
                wait("X%d" % j, 16 * NCH)
                last = None
                for c in cs4:
                    last = e.tensor_copy(
                        out=xb[:, (c % 4) * TT:(c % 4 + 1) * TT],
                        in_=OUT[:, c * D + j * TT: c * D + (j + 1) * TT])
                inc(last, "VE")
            note("VE")
            L[("p1cast", j, h)] = C["VE"]

            if mine("scalar"):
                if h == 1:
                    wait("PE", L[("p1mm", j, 0)])
                elif j >= 2:
                    wait("PE", L[("p1mm", j - 2, 1)])
                wait("X%d" % j, 16 * NCH)
                last = None
                for c in cs4:
                    last = e.activation(out=xsq[:, (c % 4) * TT:(c % 4 + 1) * TT],
                                        in_=OUT[:, c * D + j * TT: c * D + (j + 1) * TT],
                                        func=AF.Square)
                inc(last, "SC")
            note("SC")
            L[("p1sq", j, h)] = C["SC"]

            if h == 0:
                pe_claim(sb0)
                pe_claim(sb0 + 1)
            if mine("tensor"):
                wait("SC", L[("p1sq", j, h)])
                wait("VE", L[("p1cast", j, h)])
                if j == 0 and h == 0:
                    wait("VE", L["ones"])
                last = None
                for c in cs4:
                    last = e.matmul(bank_ap(sb0, 1), T["ONESC"][:, 0:1],
                                    xb[:, (c % 4) * TT:(c % 4 + 1) * TT],
                                    start=(c == 0), stop=(c == 7))
                for c in cs4:
                    last = e.matmul(bank_ap(sb0 + 1, 1), T["ONESC"][:, 0:1],
                                    xsq[:, (c % 4) * TT:(c % 4 + 1) * TT],
                                    start=(c == 0), stop=(c == 7))
                inc(last, "PE")
            note("PE")
            L[("p1mm", j, h)] = C["PE"]

        if mine("scalar"):
            wait("PE", L[("p1mm", j, 1)])
            if j >= 2:
                wait("VE", L[("p1rows", j // 2 - 1)])
            e.activation(out=scr((j % 2) * 512), in_=bank_ap(sb0, 1),
                         func=AF.Copy, scale=1.0 / 1024.0)
            last = e.activation(out=scr(2048 + (j % 2) * 512),
                                in_=bank_ap(sb0 + 1, 1),
                                func=AF.Copy, scale=1.0 / 1024.0)
            inc(last, "SC")
        note("SC")
        L[("p1evac", j)] = C["SC"]
        bank_read_done(sb0, "SC")
        bank_read_done(sb0 + 1, "SC")

    def p1_rows(b):
        if mine("vector"):
            wait("SC", L[("p1evac", 2 * b + 1)])
            e.tensor_scalar(out=ROWS[32:33, b * 1024:(b + 1) * 1024],
                            in0=scr(0, 1024), scalar1=-1.0, scalar2=None,
                            op0=OP.mult)
            C["CH"] += 1
            e.tensor_tensor(out=scr(0, 1024), in0=scr(0, 1024),
                            in1=scr(0, 1024), op=OP.mult).then_inc(S["CH"], 1)
            e.wait_ge(S["CH"], C["CH"])
            last = e.tensor_tensor(out=scr(2048, 1024), in0=scr(2048, 1024),
                                   in1=scr(0, 1024), op=OP.subtract)
            inc(last, "VE")
        else:
            C["CH"] += 1
        note("VE")
        L[("p1var", b)] = C["VE"]

        if mine("scalar"):
            wait("VE", L[("p1var", b)])
            last = e.activation(out=scr(0, 1024), in_=scr(2048, 1024),
                                func=AF.Sqrt, bias=misc(M_EPS))
            inc(last, "SC")
        note("SC")
        L[("p1sqrt", b)] = C["SC"]

        if mine("vector"):
            wait("SC", L[("p1sqrt", b)])
            last = e.reciprocal(out=ROWS[0:1, b * 1024:(b + 1) * 1024],
                                in_=scr(0, 1024))
            inc(last, "VE")
        note("VE")
        L[("p1rows", b)] = C["VE"]

    def p1_norm(j):
        bb0 = 2 if j % 2 == 0 else 6
        pe_claim(bb0)
        pe_claim(bb0 + 1)
        if mine("tensor"):
            wait("VE", L[("p1rows", j // 2)])
            e.matmul(bank_ap(bb0), T["ONESB"][32:33, :],
                     ROWS[32:33, j * TT:(j + 1) * TT], start=True, stop=True)
            last = e.matmul(bank_ap(bb0 + 1), T["ONESB"][0:1, :],
                            ROWS[0:1, j * TT:(j + 1) * TT], start=True, stop=True)
            inc(last, "PE")
        note("PE")
        L[("p1bc", j)] = C["PE"]

        if mine("vector"):
            wait("PE", L[("p1bc", j)])
            for c in range(NCH):
                e.tensor_tensor(out=OUT[:, c * D + j * TT: c * D + (j + 1) * TT],
                                in0=OUT[:, c * D + j * TT: c * D + (j + 1) * TT],
                                in1=bank_ap(bb0), op=OP.add)
            last = None
            for c in range(NCH):
                last = e.scalar_tensor_tensor(
                    out=OUT[:, c * D + j * TT: c * D + (j + 1) * TT],
                    in0=OUT[:, c * D + j * TT: c * D + (j + 1) * TT],
                    scalar=1.0, in1=bank_ap(bb0 + 1),
                    op0=OP.mult, op1=OP.mult,
                    accum_out=GCOLS[:, c * 8 + j: c * 8 + j + 1])
            if j == NT - 1:
                for c in range(NCH):
                    last = e.tensor_copy(out=T["XNCOL"][:, c:c + 1],
                                         in_=OUT[:, c * D + CS: c * D + CS + 1])
            inc(last, "VE")
        note("VE")
        L[("p1norm", j)] = C["VE"]
        bank_read_done(bb0, "VE")
        bank_read_done(bb0 + 1, "VE")

    for j in range(NT):
        if j >= 2 and j % 2 == 0:
            p1_rows(j // 2 - 1)
        p1_stageA(j)
        if j >= 2:
            p1_norm(j - 2)
    p1_rows(3)
    p1_norm(NT - 2)
    p1_norm(NT - 1)

    if stage == "xn":
        _dump_out_and_finish(me, e, T, S, C, L)
        return

    for mc in range(8):
        b = 4 + (mc % 4)
        pe_claim(b)
        if mine("tensor"):
            if mc == 0:
                wait("WD", L["wvload"])
                wait("VE", L[("p1norm", NT - 1)])
            last = None
            for kc in range(8):
                last = e.matmul(bank_ap(b, 128, 1),
                                T["WVT"][:, kc * 1024 + mc * 128: kc * 1024 + (mc + 1) * 128],
                                T["XNCOL"][:, kc:kc + 1],
                                start=(kc == 0), stop=(kc == 7))
            inc(last, "PE")
        note("PE")
        L[("vmm", mc)] = C["PE"]
        if mine("scalar"):
            wait("PE", L[("vmm", mc)])
            last = e.activation(out=T["VB"][:, mc:mc + 1],
                                in_=bank_ap(b, 128, 1), func=AF.Copy)
            inc(last, "SC")
        note("SC")
        L[("vevac", mc)] = C["SC"]
        bank_read_done(b, "SC")

    if mine("gpsimd"):
        wait("PE", L[("p1mm", NT - 1, 1)])
        for c in range(8):
            inc(e.dma_start(out=T["GC1"][:, c * 512:(c + 1) * 512],
                            in_=T["g1_d"][c * 128:(c + 1) * 128, :]), "WD", 16)
        for c in range(4):
            inc(e.dma_start(out=T["GC2"][:, c * 256:(c + 1) * 256],
                            in_=T["g2_d"][c * 128:(c + 1) * 128, :]), "WD", 16)
        for c in range(2):
            inc(e.dma_start(out=T["GC3"][:, c:c + 1],
                            in_=T["g3_d"][c * 128:(c + 1) * 128, :]), "WD", 16)
    note("WD", 16 * 14)
    L["gcw"] = C["WD"]

    if mine("vector"):
        last = None
        for c in range(NCH):
            last = e.tensor_reduce(out=MISC[:, M_SCR + c:M_SCR + c + 1],
                                   in_=GCOLS[:, c * 8:(c + 1) * 8],
                                   axis=AX.X, op=OP.add)
        C["CH"] += 1
        last.then_inc(S["CH"], 1)
        e.wait_ge(S["CH"], C["CH"])
        for c in range(NCH):
            last = e.tensor_scalar(out=T["GMEANB"][:, c:c + 1],
                                   in0=MISC[:, M_SCR + c:M_SCR + c + 1],
                                   scalar1=1.0 / 4096.0, scalar2=None, op0=OP.mult)
        inc(last, "VE")
    else:
        C["CH"] += 1
    note("VE")
    L["gmean"] = C["VE"]

    for b in (4, 5, 6, 7):
        pe_claim(b)
    if mine("tensor"):
        wait("VE", L["gmean"])
        wait("WD", L["gcw"])
        last = None
        for mc in range(4):
            for kc in range(8):
                last = e.matmul(bank_ap(4 + mc, 128, 1),
                                T["GC1"][:, kc * 512 + mc * 128: kc * 512 + (mc + 1) * 128],
                                T["GMEANB"][:, kc:kc + 1],
                                start=(kc == 0), stop=(kc == 7))
        inc(last, "PE")
    note("PE")
    L["g1mm"] = C["PE"]

    if mine("scalar"):
        wait("PE", L["g1mm"])
        wait("B")
        last = None
        for mc in range(4):
            last = e.activation(out=T["G1B"][:, mc:mc + 1],
                                in_=bank_ap(4 + mc, 128, 1), func=AF.Gelu,
                                bias=T["GB1"][:, mc:mc + 1])
        inc(last, "SC")
    note("SC")
    L["g1evac"] = C["SC"]
    for b in (4, 5, 6, 7):
        bank_read_done(b, "SC")

    pe_claim(0)
    pe_claim(1)
    if mine("tensor"):
        wait("SC", L["g1evac"])
        last = None
        for mc in range(2):
            for kc in range(4):
                last = e.matmul(bank_ap(mc, 128, 1),
                                T["GC2"][:, kc * 256 + mc * 128: kc * 256 + (mc + 1) * 128],
                                T["G1B"][:, kc:kc + 1],
                                start=(kc == 0), stop=(kc == 3))
        inc(last, "PE")
    note("PE")
    L["g2mm"] = C["PE"]

    if mine("scalar"):
        wait("PE", L["g2mm"])
        last = None
        for mc in range(2):
            last = e.activation(out=T["G2B"][:, mc:mc + 1],
                                in_=bank_ap(mc, 128, 1), func=AF.Gelu,
                                bias=T["GB2"][:, mc:mc + 1])
        inc(last, "SC")
    note("SC")
    L["g2evac"] = C["SC"]
    bank_read_done(0, "SC")
    bank_read_done(1, "SC")

    pe_claim(2)
    if mine("tensor"):
        wait("SC", L["g2evac"])
        last = None
        for kc in range(2):
            last = e.matmul(bank_ap(2, 1, 1), T["GC3"][:, kc:kc + 1],
                            T["G2B"][:, kc:kc + 1], start=(kc == 0), stop=(kc == 1))
        inc(last, "PE")
    note("PE")
    L["g3mm"] = C["PE"]

    if mine("scalar"):
        wait("PE", L["g3mm"])
        last = e.activation(out=scr(S_GF, 1), in_=bank_ap(2, 1, 1),
                            func=AF.Sigmoid, bias=misc(M_GB3))
        inc(last, "SC")
    note("SC")
    L["gf"] = C["SC"]
    bank_read_done(2, "SC")

    if mine("vector"):
        wait("SC", L["gf"])
        def _step(inst):
            C["CH"] += 1
            inst.then_inc(S["CH"], 1)
            e.wait_ge(S["CH"], C["CH"])
        _step(e.tensor_scalar(out=scr(S_K2, 1), in0=scr(S_AR, 1), scalar1=0.6,
                              scalar2=None, op0=OP.mult))
        _step(e.tensor_scalar(out=scr(S_T11, 1), in0=scr(S_GF, 1),
                              scalar1=scr(S_AR, 1), scalar2=None, op0=OP.mult))
        _step(e.tensor_scalar(out=scr(S_T11, 1), in0=scr(S_T11, 1), scalar1=1.4,
                              scalar2=0.1, op0=OP.mult, op1=OP.add))
        _step(e.tensor_tensor(out=scr(S_K1, 1), in0=scr(S_T11, 1),
                              in1=scr(S_AR, 1), op=OP.subtract))
        inc(e.engine_nop(), "VE")
    else:
        C["CH"] += 4
    note("VE")
    L["kconst"] = C["VE"]

    if mine("gpsimd"):
        wait("PE", L["g3mm"])
        for c in range(8):
            inc(e.dma_start(out=T["WOT"][:, c * 1024:(c + 1) * 1024],
                            in_=T["wot_d"][c * 128:(c + 1) * 128, :]), "WD", 16)
    note("WD", 16 * 8)
    L["wvw"] = C["WD"]

    for mc in range(8):
        b = mc % 4
        pe_claim(b)
        if mine("tensor"):
            if mc == 0:
                wait("SC", L[("vevac", 7)])
                wait("WD", L["wvw"])
            last = None
            for kc in range(8):
                last = e.matmul(bank_ap(b, 128, 1),
                                T["WOT"][:, kc * 1024 + mc * 128: kc * 1024 + (mc + 1) * 128],
                                T["VB"][:, kc:kc + 1],
                                start=(kc == 0), stop=(kc == 7))
            inc(last, "PE")
        note("PE")
        L[("rmm", mc)] = C["PE"]
        if mine("scalar"):
            wait("PE", L[("rmm", mc)])
            last = e.activation(out=T["RSB"][:, mc:mc + 1],
                                in_=bank_ap(b, 128, 1), func=AF.Copy)
            inc(last, "SC")
        note("SC")
        L[("revac", mc)] = C["SC"]
        bank_read_done(b, "SC")

    if mine("gpsimd"):
        wait("PE", L[("rmm", 7)])
        for c in range(8):
            inc(e.dma_start(out=T["W1"][:, c * 1024:(c + 1) * 1024],
                            in_=T["w1_d"][c * 128:(c + 1) * 128, :]), "W2", 16)
        for c in range(8):
            inc(e.dma_start(out=T["W2"][:, c * 512:(c + 1) * 512],
                            in_=T["w2_d"][c * 128:(c + 1) * 128, :]), "W2", 16)
        for c in range(4):
            inc(e.dma_start(out=T["W3"][:, c * 256:(c + 1) * 256],
                            in_=T["w3_d"][c * 128:(c + 1) * 128, :]), "W2", 16)
        for c in range(2):
            inc(e.dma_start(out=T["W4"][:, c:c + 1],
                            in_=T["w4_d"][c * 128:(c + 1) * 128, :]), "W2", 16)
    note("W2", 16 * 22)
    L["cnw"] = C["W2"]

    n_cycles = {"cycle1": 1, "cycle2": 2, "cycle3": 3, "full": 3}[stage]

    def emit_apply(pcyc, ptj):
        if mine("vector"):
            wait("SC", L[("cpevac", pcyc, ptj)])
            last = None
            for q in range(4):
                e.tensor_tensor(
                    out=T["TMP"][:, 0:TT],
                    in0=OUT[:, TWIST_SRC[q] * D + ptj * TT:
                            TWIST_SRC[q] * D + (ptj + 1) * TT],
                    in1=T["CPS"][:, (ptj % 4) * TT:(ptj % 4 + 1) * TT],
                    op=OP.mult)
                last = e.tensor_tensor(
                    out=T["TMP"][:, TT:2 * TT],
                    in0=OUT[:, q * D + ptj * TT: q * D + (ptj + 1) * TT],
                    in1=T["CPS"][:, (ptj % 4) * TT:(ptj % 4 + 1) * TT],
                    op=OP.mult)
                if q == 3:
                    inc(last, "VE")
                e.tensor_tensor(
                    out=OUT[:, q * D + ptj * TT: q * D + (ptj + 1) * TT],
                    in0=OUT[:, q * D + ptj * TT: q * D + (ptj + 1) * TT],
                    in1=T["TMP"][:, 0:TT], op=TWIST_OP[q])
                last = e.tensor_tensor(
                    out=OUT[:, (q + 4) * D + ptj * TT:
                            (q + 4) * D + (ptj + 1) * TT],
                    in0=OUT[:, (q + 4) * D + ptj * TT:
                            (q + 4) * D + (ptj + 1) * TT],
                    in1=T["TMP"][:, TT:2 * TT], op=TWIST_OP[q + 4])
            inc(last, "VE")
        note("VE", 2)
        L[("applym", pcyc, ptj)] = C["VE"] - 1
        L[("apply", pcyc, ptj)] = C["VE"]

    def emit_cpevac(pcyc, ptj, i, pp):
        b = BC_BANK[i]
        if mine("scalar"):
            wait("PE", L[("bc", pcyc, ptj)])
            if pp is not None:
                wait("VE", L[("applym", pp[0], pp[1] + i)])
            last = e.activation(out=T["CPS"][:, i * TT:(i + 1) * TT],
                                in_=bank_ap(b), func=AF.Copy)
            inc(last, "SC")
        note("SC")
        L[("cpevac", pcyc, ptj)] = C["SC"]
        bank_read_done(b, "SC")

    def half_before(cyc, half):
        return (cyc, 0) if half == 1 else ((cyc - 1, 1) if cyc > 0 else None)

    for cyc in range(n_cycles):
        for half in range(2):
            tiles = [half * 4 + i for i in range(4)]
            prev = half_before(cyc, half)

            if prev is not None:
                pp = half_before(*prev)
                for i in range(4):
                    emit_cpevac(prev[0], prev[1] * 4 + i, i,
                                (pp[0], pp[1] * 4) if pp is not None else None)

            for i, tj in enumerate(tiles):
                if mine("vector"):
                    pg = tj - 1 if tj > 0 else (NT - 1 if cyc > 0 else None)
                    if pg is not None:
                        pc = cyc if tj > 0 else cyc - 1
                        wait("PE", L[("h1mm", pc, pg, 7)])
                    last = None
                    for c in range(NCH):
                        last = e.tensor_copy(
                            out=T["OUTB"][:, c * TT:(c + 1) * TT],
                            in_=OUT[:, c * D + tj * TT: c * D + (tj + 1) * TT])
                    inc(last, "VE")
                note("VE")
                L[("cast", cyc, tj)] = C["VE"]

                if prev is not None:
                    emit_apply(prev[0], prev[1] * 4 + i)
                for mc in range(8):
                    pe_claim(mc)
                    if mine("tensor"):
                        if mc == 0:
                            wait("VE", L[("cast", cyc, tj)])
                            if cyc == 0 and tj == 0:
                                wait("W2", L["cnw"])
                        last = None
                        for kc in range(8):
                            last = e.matmul(
                                bank_ap(mc),
                                T["W1"][:, kc * 1024 + mc * 128: kc * 1024 + (mc + 1) * 128],
                                T["OUTB"][:, kc * TT:(kc + 1) * TT],
                                start=(kc == 0), stop=(kc == 7))
                        inc(last, "PE")
                    note("PE")
                    L[("h1mm", cyc, tj, mc)] = C["PE"]

                for mc in range(8):
                    if mine("scalar"):
                        wait("PE", L[("h1mm", cyc, tj, mc)])
                        last = e.activation(out=T["H1B"][:, mc * TT:(mc + 1) * TT],
                                            in_=bank_ap(mc), func=AF.Gelu,
                                            bias=T["B1"][:, mc:mc + 1])
                        inc(last, "SC")
                    note("SC")
                    L[("h1evac", cyc, tj, mc)] = C["SC"]
                    bank_read_done(mc, "SC")

                for mc in range(4):
                    pe_claim(mc)
                    if mine("tensor"):
                        if mc == 0:
                            wait("SC", L[("h1evac", cyc, tj, 7)])
                        last = None
                        for kc in range(8):
                            last = e.matmul(
                                bank_ap(mc),
                                T["W2"][:, kc * 512 + mc * 128: kc * 512 + (mc + 1) * 128],
                                T["H1B"][:, kc * TT:(kc + 1) * TT],
                                start=(kc == 0), stop=(kc == 7))
                        inc(last, "PE")
                    note("PE")
                    L[("h2mm", cyc, tj, mc)] = C["PE"]

                for mc in range(4):
                    if mine("scalar"):
                        wait("PE", L[("h2mm", cyc, tj, mc)])
                        last = e.activation(out=T["H2B"][:, mc * TT:(mc + 1) * TT],
                                            in_=bank_ap(mc), func=AF.Gelu,
                                            bias=T["B2"][:, mc:mc + 1])
                        inc(last, "SC")
                    note("SC")
                    L[("h2evac", cyc, tj, mc)] = C["SC"]
                    bank_read_done(mc, "SC")

                for mc in range(2):
                    pe_claim(4 + mc)
                    if mine("tensor"):
                        if mc == 0:
                            wait("SC", L[("h2evac", cyc, tj, 3)])
                        last = None
                        for kc in range(4):
                            last = e.matmul(
                                bank_ap(4 + mc),
                                T["W3"][:, kc * 256 + mc * 128: kc * 256 + (mc + 1) * 128],
                                T["H2B"][:, kc * TT:(kc + 1) * TT],
                                start=(kc == 0), stop=(kc == 3))
                        inc(last, "PE")
                    note("PE")
                    L[("h3mm", cyc, tj, mc)] = C["PE"]

                for mc in range(2):
                    if mine("scalar"):
                        wait("PE", L[("h3mm", cyc, tj, mc)])
                        last = e.activation(out=T["H3B"][:, mc * TT:(mc + 1) * TT],
                                            in_=bank_ap(4 + mc), func=AF.Gelu,
                                            bias=T["B3"][:, mc:mc + 1])
                        inc(last, "SC")
                    note("SC")
                    L[("h3evac", cyc, tj, mc)] = C["SC"]
                    bank_read_done(4 + mc, "SC")

                hb = 6 + (i % 2)
                pe_claim(hb)
                if mine("tensor"):
                    wait("SC", L[("h3evac", cyc, tj, 1)])
                    last = None
                    for kc in range(2):
                        last = e.matmul(bank_ap(hb, 1), T["W4"][:, kc:kc + 1],
                                        T["H3B"][:, kc * TT:(kc + 1) * TT],
                                        start=(kc == 0), stop=(kc == 1))
                    inc(last, "PE")
                note("PE")
                L[("h4", cyc, tj)] = C["PE"]

                if mine("scalar"):
                    wait("PE", L[("h4", cyc, tj)])
                    last = e.activation(out=scr(H4R + i * 512),
                                        in_=bank_ap(hb, 1), func=AF.Copy)
                    inc(last, "SC")
                note("SC")
                L[("h4evac", cyc, tj)] = C["SC"]
                bank_read_done(hb, "SC")

            if mine("scalar"):
                if prev is not None:
                    wait("VE", L[("cprow",) + prev])
                last = e.activation(out=scr(SIGR, 2048), in_=scr(H4R, 2048),
                                    func=AF.Sigmoid, bias=misc(M_B4))
                inc(last, "SC")
            note("SC")
            L[("sig", cyc, half)] = C["SC"]

            if mine("vector"):
                wait("SC", L[("sig", cyc, half)])
                if cyc == 0 and half == 0:
                    wait("VE", L["kconst"])
                last = e.tensor_scalar(out=scr(H4R, 2048), in0=scr(SIGR, 2048),
                                       scalar1=scr(S_K2, 1), scalar2=scr(S_K1, 1),
                                       op0=OP.mult, op1=OP.add)
                inc(last, "VE")
            note("VE")
            L[("cprow", cyc, half)] = C["VE"]

            for i, tj in enumerate(tiles):
                b = BC_BANK[i]
                pe_claim(b)
                if mine("tensor"):
                    if i == 0:
                        wait("VE", L[("cprow", cyc, half)])
                    last = e.matmul(bank_ap(b), T["ONESB"][64:65, :],
                                    scr(H4R + i * 512), start=True, stop=True)
                    inc(last, "PE")
                note("PE")
                L[("bc", cyc, tj)] = C["PE"]

    lastc = n_cycles - 1
    for i in range(4):
        emit_cpevac(lastc, 4 + i, i, (lastc, 0))
    for i in range(4):
        emit_apply(lastc, 4 + i)

    if stage in ("cycle1", "cycle2", "cycle3"):
        _dump_out_and_finish(me, e, T, S, C, L,
                             after=("apply", lastc, NT - 1))
        return

    for j in range(NT):
        xt = T["XT0"] if j % 2 == 0 else T["XT1"]
        bnm = 2 if j % 2 == 0 else 6
        brs = bnm + 1
        if mine("gpsimd"):
            if j <= 1:
                wait("PE", L[("bc", 2, NT - 1)])
                wait("SC", L[("h4evac", 2, NT - 1)])
            else:
                wait("ST", 16 * (j - 1))
            xsrc = T["x_d"].rearrange("(c p) t -> p c t", p=128)
            inc(e.dma_start(out=xt[:, :].rearrange("p (c t) -> p c t", c=NCH),
                            in_=xsrc[:, :, j * TT:(j + 1) * TT]),
                "XTA" if j % 2 == 0 else "XTB", 16)
        xsem = "XTA" if j % 2 == 0 else "XTB"
        note(xsem, 16)
        L[("xt", j)] = (xsem, C[xsem])

        pe_claim(brs)
        if mine("tensor"):
            wait("VE", L[("apply", 2, j)])
            last = e.matmul(bank_ap(brs), T["ONESB"][0:1, :],
                            ROWS[0:1, j * TT:(j + 1) * TT], start=True, stop=True)
            inc(last, "PE")
        note("PE")
        L[("p3bc", j)] = C["PE"]

        pe_claim(bnm)
        if mine("tensor"):
            last = e.matmul(bank_ap(bnm), T["ONESB"][32:33, :],
                            ROWS[32:33, j * TT:(j + 1) * TT], start=True, stop=True)
            inc(last, "PE")
        note("PE")
        L[("p3w", j)] = C["PE"]

        for h in range(2):
            ch = range(4 * h, 4 * h + 4)
            if mine("scalar"):
                if j == 0 and h == 0:
                    wait("VE", L[("apply", 2, NT - 1)])
                elif not (j == 0 and h == 0):
                    pj, ph = (j, h - 1) if h == 1 else (j - 1, 1)
                    if (pj, ph) != (0, -1):
                        wait("VE", L[("p3half", pj, ph)])
                last = None
                for c in ch:
                    last = e.activation(out=T["GT"][:, c * TT:(c + 1) * TT],
                                        in_=OUT[:, c * D + j * TT: c * D + (j + 1) * TT],
                                        func=AF.Silu)
                inc(last, "SC")
            note("SC")
            L[("p3silu", j, h)] = C["SC"]

            if mine("vector"):
                if h == 0:
                    wait(*L[("xt", j)])
                    wait("PE", L[("p3w", j)])
                wait("SC", L[("p3silu", j, h)])
                last = None
                W = 384 if j == NT - 1 else TT
                for c in ch:
                    if j == NT - 1:
                        e.tensor_scalar(
                            out=T["TMP"][:, TT + 384:TT + 512],
                            in0=T["GT"][:, c * TT + 384: c * TT + 512],
                            scalar1=T["RSB"][:, c:c + 1], scalar2=None, op0=OP.mult)
                    e.tensor_tensor(out=T["GT"][:, c * TT:c * TT + W],
                                    in0=T["GT"][:, c * TT:c * TT + W],
                                    in1=bank_ap(brs, 128, W), op=OP.mult)
                    last = e.tensor_tensor(out=T["TMP"][:, 0:W],
                                           in0=xt[:, c * TT:c * TT + W],
                                           in1=bank_ap(bnm, 128, W), op=OP.add)
                    if c == NCH - 1:
                        inc(last, "VE")
                    e.tensor_tensor(out=T["TMP"][:, 0:W],
                                    in0=T["TMP"][:, 0:W],
                                    in1=T["GT"][:, c * TT:c * TT + W], op=OP.mult)
                    last = e.tensor_tensor(out=xt[:, c * TT:c * TT + W],
                                           in0=xt[:, c * TT:c * TT + W],
                                           in1=T["TMP"][:, 0:W], op=OP.add)
                    if j == NT - 1:
                        last = e.tensor_tensor(
                            out=xt[:, c * TT + 384: c * TT + 512],
                            in0=xt[:, c * TT + 384: c * TT + 512],
                            in1=T["TMP"][:, TT + 384:TT + 512], op=OP.add)
                inc(last, "VE")
            note("VE", 2 if h == 1 else 1)
            L[("p3half", j, h)] = C["VE"]
        L[("p3u", j)] = C["VE"] - 1
        L[("p3dve", j)] = C["VE"]
        bank_read_done(bnm, "VE")
        bank_read_done(brs, "VE")

        if mine("sync"):
            wait("VE", L[("p3dve", j)])
            ydst = T["y_d"].rearrange("(c p) t -> p c t", p=128)
            inc(e.dma_start(out=ydst[:, :, j * TT:(j + 1) * TT],
                            in_=xt[:, :].rearrange("p (c t) -> p c t", c=NCH)),
                "ST", 16)
        note("ST", 16)

    if mine("sync"):
        wait("ST")


def _dump_out_and_finish(me, e, T, S, C, L, after=None):
    C["ST"] += 16 * NCH
    if me == "sync":
        if after is None:
            e.wait_ge(S["VE"], L[("p1norm", NT - 1)])
        else:
            e.wait_ge(S["VE"], L[after])
        for c in range(NCH):
            e.dma_start(out=T["y_d"][c * 128:(c + 1) * 128, :],
                        in_=T["OUT"][:, c * D:(c + 1) * D]).then_inc(S["ST"], 16)
        e.wait_ge(S["ST"], C["ST"])


def build_program(stage="full"):
    nc = bass.Bass("TRN2", target_bir_lowering=False, debug=False)
    T = {}
    T["x_d"] = nc.dram_tensor("x", [1024, 4096], F32, kind="ExternalInput").ap()
    for nm, shape in (("w1", [1024, 1024]), ("w2", [1024, 512]),
                      ("w3", [512, 256]), ("w4", [256, 1]),
                      ("g1", [1024, 512]), ("g2", [512, 256]), ("g3", [256, 1]),
                      ("wvt", [1024, 1024]), ("wot", [1024, 1024]),
                      ("b1r", [128, 8]), ("b2r", [128, 4]), ("b3r", [128, 2]),
                      ("gb1r", [128, 4]), ("gb2r", [128, 2]),
                      ("b4r", [1, 1]), ("gb3r", [1, 1]), ("arr", [1, 1]),
                      ("eye", [128, 128])):
        T[nm + "_d"] = nc.dram_tensor(nm, shape, F32, kind="ExternalInput").ap()
    T["y_d"] = nc.dram_tensor("y", [1024, 4096], F32, kind="ExternalOutput").ap()

    T.update(_alloc_tensors(nc))

    from contextlib import ExitStack
    with ExitStack() as stack:
        S = {}
        for nm in ("X", "B", "WD", "W2", "PE", "SC", "VE", "ST", "XT", "CH",
                   "X0", "X1", "X2", "X3", "X4", "X5", "X6", "X7",
                   "XTA", "XTB"):
            S[nm] = stack.enter_context(nc.semaphore(nm))
        with nc.Block() as block:
            block.sync(lambda e: _emit("sync", e, nc, T, S, stage))
            block.gpsimd(lambda e: _emit("gpsimd", e, nc, T, S, stage))
            block.tensor(lambda e: _emit("tensor", e, nc, T, S, stage))
            block.scalar(lambda e: _emit("scalar", e, nc, T, S, stage))
            block.vector(lambda e: _emit("vector", e, nc, T, S, stage))
    return nc


def make_in_maps(inputs):
    x = inputs["L0_attn_inputs"]
    c = np.ascontiguousarray
    shared = {
        "w1": c(inputs["L0_ffn_cn_w1"]), "w2": c(inputs["L0_ffn_cn_w2"]),
        "w3": c(inputs["L0_ffn_cn_w3"]), "w4": c(inputs["L0_ffn_cn_w4"]),
        "g1": c(inputs["L0_ffn_gc_w1"]), "g2": c(inputs["L0_ffn_gc_w2"]),
        "g3": c(inputs["L0_ffn_gc_w3"]),
        "wvt": c(inputs["L0_attn_Wv"].T), "wot": c(inputs["L0_attn_Wo"].T),
        "b1r": c(inputs["L0_ffn_cn_b1"].reshape(8, 128).T),
        "b2r": c(inputs["L0_ffn_cn_b2"].reshape(4, 128).T),
        "b3r": c(inputs["L0_ffn_cn_b3"].reshape(2, 128).T),
        "gb1r": c(inputs["L0_ffn_gc_b1"].reshape(4, 128).T),
        "gb2r": c(inputs["L0_ffn_gc_b2"].reshape(2, 128).T),
        "b4r": c(inputs["L0_ffn_cn_b4"].reshape(1, 1)),
        "gb3r": c(inputs["L0_ffn_gc_b3"].reshape(1, 1)),
        "eye": np.eye(128, dtype=np.float32),
        "arr": c(inputs["L0_ffn_adaptive_range"].reshape(1, 1)),
    }
    in_maps = []
    for b in range(8):
        m = dict(shared)
        m["x"] = c(x[b].T)
        in_maps.append(m)
    return in_maps


def kernel(**inputs):
    nc = build_program("full")
    in_maps = make_in_maps(inputs)
    res = run_bass_kernel_spmd(nc, in_maps, core_ids=list(range(8)))
    out = np.empty((8, 4096, 1024), dtype=np.float32)
    for b in range(8):
        out[b] = res.results[b]["y"].T
    return out


# version 18
# speedup vs baseline: 1.0883x; 1.0371x over previous
import numpy as np
import concourse.bass as bass
import concourse.mybir as mybir
from concourse.bass_utils import run_bass_kernel_spmd

F32 = mybir.dt.float32
BF16 = mybir.dt.bfloat16
AF = mybir.ActivationFunctionType
OP = mybir.AluOpType
AX = mybir.AxisListType

NCH = 8
NT = 8
TT = 512
D = 4096
CS = 3968

TWIST_SRC = {0: 4, 1: 5, 2: 6, 3: 7, 4: 0, 5: 1, 6: 2, 7: 3}
TWIST_OP = {0: OP.add, 1: OP.add, 2: OP.subtract, 3: OP.subtract,
            4: OP.subtract, 5: OP.subtract, 6: OP.add, 7: OP.add}

H4R = 0
SIGR = 2048
R_M = 0
R_Q = 512
R_SQ = 1024
S_GF = 4096
S_AR = 4097
S_K1 = 4098
S_K2 = 4099
S_T11 = 4100
ROWS_COLS = 4128
M_B4, M_GB3, M_EPS = range(3)
M_SCR = 3

BC_BANK = [7, 6, 0, 1]


def _alloc_tensors(nc):
    T = {}
    base = (nc._sbuf_addr_for_side("left") + 31) & ~31
    off = [base]

    def sb(name, shape, dt, at=None):
        pos = off[0] if at is None else (base + at)
        h = nc.alloc_sbuf_tensor_at(name, shape, dt, offset=pos)
        nbytes = int(np.prod(shape[1:])) * (2 if dt == BF16 else 4)
        if at is None:
            off[0] += (nbytes + 31) & ~31
        T[name] = h

    sb("OUT", [128, NCH * D], F32)
    sb("ROWS", [128, ROWS_COLS], F32)
    sb("MISC", [128, 16], F32)
    sb("GCOLS", [128, 64], F32)
    sb("B1", [128, 8], F32)
    sb("B2", [128, 4], F32)
    sb("B3", [128, 2], F32)
    sb("GB1", [128, 4], F32)
    sb("GB2", [128, 2], F32)
    sb("GMEANB", [128, 8], BF16)
    sb("XNCOL", [128, 8], BF16)
    sb("VB", [128, 8], BF16)
    sb("RSB", [128, 8], F32)
    sb("G1B", [128, 4], BF16)
    sb("G2B", [128, 2], BF16)
    sb("ONESC", [128, 1], BF16)
    sb("ONESB", [128, 128], F32)
    sb("EYE", [128, 128], F32)
    sb("TMP", [128, 2 * TT], F32)
    sb("CPS", [128, 4 * TT], F32)
    era = off[0] - base

    sb("XB", [128, 4 * TT], BF16, at=era)
    sb("XSQ", [128, 4 * TT], BF16, at=era + 4096)
    sb("XB2", [128, 4 * TT], BF16, at=era + 8192)
    sb("XSQ2", [128, 4 * TT], BF16, at=era + 12288)
    sb("GC1", [128, 8 * 512], BF16, at=era)
    sb("GC2", [128, 4 * 256], BF16, at=era + 8192)
    sb("GC3", [128, 2], BF16, at=era + 10240)
    sb("WVT", [128, 8 * 1024], BF16, at=era + 16384)
    sb("WOT", [128, 8 * 1024], BF16, at=era)
    sb("W1", [128, 8 * 1024], BF16, at=era)
    sb("W2", [128, 8 * 512], BF16, at=era + 16384)
    sb("W3", [128, 4 * 256], BF16, at=era + 24576)
    sb("W4", [128, 2], BF16, at=era + 26624)
    sb("H1B", [128, 8 * TT], BF16, at=era + 26656)
    sb("H2B", [128, 4 * TT], BF16, at=era + 34848)
    sb("H3B", [128, 2 * TT], BF16, at=era + 38944)
    sb("OUTB", [128, 8 * TT], BF16, at=era + 40992)
    era2_end = era + 40992 + 8192
    sb("XT0", [128, NCH * TT], F32, at=era)
    sb("GT", [128, NCH * TT], F32, at=era + 16384)
    sb("XT1", [128, NCH * TT], F32, at=era + 32768)
    high = base + max(era2_end, era + 49152)
    assert high <= 229344, high

    T["pA"] = nc.alloc_psum_tensor("pA", [128, 2048], F32)
    T["pB"] = nc.alloc_psum_tensor("pB", [128, 2048], F32)
    return T


def _emit(me, e, nc, T, S, stage):
    C = {k: 0 for k in S}
    L = {}
    OUT, ROWS, MISC, GCOLS = T["OUT"], T["ROWS"], T["MISC"], T["GCOLS"]
    pA, pB = T["pA"], T["pB"]

    def inc(inst, sem, n=1):
        if inst is not None:
            inst.then_inc(S[sem], n)

    def note(sem, n=1):
        C[sem] += n

    def wait(sem, val=None):
        v = C[sem] if val is None else val
        if v > 0:
            e.wait_ge(S[sem], v)

    def scr(o, n=512):
        return ROWS[64:65, o:o + n]

    def misc(col):
        return MISC[0:1, col:col + 1]

    mine = lambda name: me == name

    def bank_ap(b, rows=128, n=512):
        ps = pA if b < 4 else pB
        o = (b % 4) * 512
        return ps[0:rows, o:o + n]

    bank_free = {b: None for b in range(8)}

    def pe_claim(b):
        w = bank_free[b]
        if w is not None and mine("tensor"):
            wait(w[0], w[1])

    def bank_read_done(b, sem):
        bank_free[b] = (sem, C[sem])

    if mine("sync"):
        for j in range(NT):
            for c in range(NCH):
                inc(e.dma_start(out=OUT[:, c * D + j * TT: c * D + (j + 1) * TT],
                                in_=T["x_d"][c * 128:(c + 1) * 128,
                                             j * TT:(j + 1) * TT]),
                    "X%d" % j, 16)
        for nm, dst in (("b1r", T["B1"]), ("b2r", T["B2"]), ("b3r", T["B3"]),
                        ("gb1r", T["GB1"]), ("gb2r", T["GB2"])):
            inc(e.dma_start(out=dst[:, :], in_=T[nm + "_d"][:, :]), "B", 16)
        for nm, colo in (("b4r", M_B4), ("gb3r", M_GB3)):
            inc(e.dma_start(out=misc(colo), in_=T[nm + "_d"][:, :]), "B", 16)
        inc(e.dma_start(out=ROWS[64:65, S_AR:S_AR + 1],
                        in_=T["arr_d"][:, :]), "B", 16)
        inc(e.dma_start(out=T["EYE"][:, :], in_=T["eye_d"][:, :]), "B", 16)
    for j in range(NT):
        note("X%d" % j, 16 * NCH)
    note("B", 16 * 9)

    if mine("gpsimd"):
        for c in range(8):
            inc(e.dma_start(out=T["WVT"][:, c * 1024:(c + 1) * 1024],
                            in_=T["wvt_d"][c * 128:(c + 1) * 128, :]), "WD", 16)
    note("WD", 16 * 8)
    L["wvload"] = C["WD"]

    if mine("vector"):
        e.memset(T["ONESC"][:, :], 1.0)
        e.memset(MISC[0:1, M_EPS:M_EPS + 1], 1e-5)
        i2 = e.memset(T["ONESB"][:, :], 1.0)
        inc(i2, "VE")
    note("VE")
    L["ones"] = C["VE"]

    def p1_stageA(j):
        sb0 = 0 if j % 2 == 0 else 4
        xb = T["XB"] if j % 2 == 0 else T["XB2"]
        xsq = T["XSQ"] if j % 2 == 0 else T["XSQ2"]
        for h in range(2):
            cs4 = range(4 * h, 4 * h + 4)
            if mine("vector"):
                if h == 1:
                    wait("PE", L[("p1mm", j, 0)])
                elif j >= 2:
                    wait("PE", L[("p1mm", j - 2, 1)])
                wait("X%d" % j, 16 * NCH)
                last = None
                for c in cs4:
                    last = e.tensor_copy(
                        out=xb[:, (c % 4) * TT:(c % 4 + 1) * TT],
                        in_=OUT[:, c * D + j * TT: c * D + (j + 1) * TT])
                inc(last, "VE")
            note("VE")
            L[("p1cast", j, h)] = C["VE"]

            if mine("scalar"):
                if h == 1:
                    wait("PE", L[("p1mm", j, 0)])
                elif j >= 2:
                    wait("PE", L[("p1mm", j - 2, 1)])
                wait("X%d" % j, 16 * NCH)
                last = None
                for c in cs4:
                    last = e.activation(out=xsq[:, (c % 4) * TT:(c % 4 + 1) * TT],
                                        in_=OUT[:, c * D + j * TT: c * D + (j + 1) * TT],
                                        func=AF.Square)
                inc(last, "SC")
            note("SC")
            L[("p1sq", j, h)] = C["SC"]

            if h == 0:
                pe_claim(sb0)
                pe_claim(sb0 + 1)
            if mine("tensor"):
                wait("SC", L[("p1sq", j, h)])
                wait("VE", L[("p1cast", j, h)])
                if j == 0 and h == 0:
                    wait("VE", L["ones"])
                last = None
                for c in cs4:
                    last = e.matmul(bank_ap(sb0, 1), T["ONESC"][:, 0:1],
                                    xb[:, (c % 4) * TT:(c % 4 + 1) * TT],
                                    start=(c == 0), stop=(c == 7))
                for c in cs4:
                    last = e.matmul(bank_ap(sb0 + 1, 1), T["ONESC"][:, 0:1],
                                    xsq[:, (c % 4) * TT:(c % 4 + 1) * TT],
                                    start=(c == 0), stop=(c == 7))
                inc(last, "PE")
            note("PE")
            L[("p1mm", j, h)] = C["PE"]

        if mine("scalar"):
            wait("PE", L[("p1mm", j, 1)])
            if j >= 2:
                wait("VE", L[("p1rows", j // 2 - 1)])
            e.activation(out=scr((j % 2) * 512), in_=bank_ap(sb0, 1),
                         func=AF.Copy, scale=1.0 / 1024.0)
            last = e.activation(out=scr(2048 + (j % 2) * 512),
                                in_=bank_ap(sb0 + 1, 1),
                                func=AF.Copy, scale=1.0 / 1024.0)
            inc(last, "SC")
        note("SC")
        L[("p1evac", j)] = C["SC"]
        bank_read_done(sb0, "SC")
        bank_read_done(sb0 + 1, "SC")

    def p1_rows(b):
        if mine("vector"):
            wait("SC", L[("p1evac", 2 * b + 1)])
            e.tensor_scalar(out=ROWS[32:33, b * 1024:(b + 1) * 1024],
                            in0=scr(0, 1024), scalar1=-1.0, scalar2=None,
                            op0=OP.mult)
            C["CH"] += 1
            e.tensor_tensor(out=scr(0, 1024), in0=scr(0, 1024),
                            in1=scr(0, 1024), op=OP.mult).then_inc(S["CH"], 1)
            e.wait_ge(S["CH"], C["CH"])
            last = e.tensor_tensor(out=scr(2048, 1024), in0=scr(2048, 1024),
                                   in1=scr(0, 1024), op=OP.subtract)
            inc(last, "VE")
        else:
            C["CH"] += 1
        note("VE")
        L[("p1var", b)] = C["VE"]

        if mine("scalar"):
            wait("VE", L[("p1var", b)])
            last = e.activation(out=scr(0, 1024), in_=scr(2048, 1024),
                                func=AF.Sqrt, bias=misc(M_EPS))
            inc(last, "SC")
        note("SC")
        L[("p1sqrt", b)] = C["SC"]

        if mine("vector"):
            wait("SC", L[("p1sqrt", b)])
            last = e.reciprocal(out=ROWS[0:1, b * 1024:(b + 1) * 1024],
                                in_=scr(0, 1024))
            inc(last, "VE")
        note("VE")
        L[("p1rows", b)] = C["VE"]

    def p1_norm(j):
        bb0 = 2 if j % 2 == 0 else 6
        pe_claim(bb0)
        pe_claim(bb0 + 1)
        if mine("tensor"):
            wait("VE", L[("p1rows", j // 2)])
            e.matmul(bank_ap(bb0), T["ONESB"][32:33, :],
                     ROWS[32:33, j * TT:(j + 1) * TT], start=True, stop=True)
            last = e.matmul(bank_ap(bb0 + 1), T["ONESB"][0:1, :],
                            ROWS[0:1, j * TT:(j + 1) * TT], start=True, stop=True)
            inc(last, "PE")
        note("PE")
        L[("p1bc", j)] = C["PE"]

        if mine("vector"):
            wait("PE", L[("p1bc", j)])
            for c in range(NCH):
                e.tensor_tensor(out=OUT[:, c * D + j * TT: c * D + (j + 1) * TT],
                                in0=OUT[:, c * D + j * TT: c * D + (j + 1) * TT],
                                in1=bank_ap(bb0), op=OP.add)
            last = None
            for c in range(NCH):
                last = e.scalar_tensor_tensor(
                    out=OUT[:, c * D + j * TT: c * D + (j + 1) * TT],
                    in0=OUT[:, c * D + j * TT: c * D + (j + 1) * TT],
                    scalar=1.0, in1=bank_ap(bb0 + 1),
                    op0=OP.mult, op1=OP.mult,
                    accum_out=GCOLS[:, c * 8 + j: c * 8 + j + 1])
            if j == NT - 1:
                for c in range(NCH):
                    last = e.tensor_copy(out=T["XNCOL"][:, c:c + 1],
                                         in_=OUT[:, c * D + CS: c * D + CS + 1])
            inc(last, "VE")
        note("VE")
        L[("p1norm", j)] = C["VE"]
        bank_read_done(bb0, "VE")
        bank_read_done(bb0 + 1, "VE")

    for j in range(NT):
        if j >= 2 and j % 2 == 0:
            p1_rows(j // 2 - 1)
        p1_stageA(j)
        if j >= 2:
            p1_norm(j - 2)
    p1_rows(3)
    p1_norm(NT - 2)
    p1_norm(NT - 1)

    if stage == "xn":
        _dump_out_and_finish(me, e, T, S, C, L)
        return

    for mc in range(8):
        b = 4 + (mc % 4)
        pe_claim(b)
        if mine("tensor"):
            if mc == 0:
                wait("WD", L["wvload"])
                wait("VE", L[("p1norm", NT - 1)])
            last = None
            for kc in range(8):
                last = e.matmul(bank_ap(b, 128, 1),
                                T["WVT"][:, kc * 1024 + mc * 128: kc * 1024 + (mc + 1) * 128],
                                T["XNCOL"][:, kc:kc + 1],
                                start=(kc == 0), stop=(kc == 7))
            inc(last, "PE")
        note("PE")
        L[("vmm", mc)] = C["PE"]
        if mine("scalar"):
            wait("PE", L[("vmm", mc)])
            last = e.activation(out=T["VB"][:, mc:mc + 1],
                                in_=bank_ap(b, 128, 1), func=AF.Copy)
            inc(last, "SC")
        note("SC")
        L[("vevac", mc)] = C["SC"]
        bank_read_done(b, "SC")

    if mine("gpsimd"):
        wait("PE", L[("p1mm", NT - 1, 1)])
        for c in range(8):
            inc(e.dma_start(out=T["GC1"][:, c * 512:(c + 1) * 512],
                            in_=T["g1_d"][c * 128:(c + 1) * 128, :]), "WD", 16)
        for c in range(4):
            inc(e.dma_start(out=T["GC2"][:, c * 256:(c + 1) * 256],
                            in_=T["g2_d"][c * 128:(c + 1) * 128, :]), "WD", 16)
        for c in range(2):
            inc(e.dma_start(out=T["GC3"][:, c:c + 1],
                            in_=T["g3_d"][c * 128:(c + 1) * 128, :]), "WD", 16)
    note("WD", 16 * 14)
    L["gcw"] = C["WD"]

    if mine("vector"):
        last = None
        for c in range(NCH):
            last = e.tensor_reduce(out=MISC[:, M_SCR + c:M_SCR + c + 1],
                                   in_=GCOLS[:, c * 8:(c + 1) * 8],
                                   axis=AX.X, op=OP.add)
        C["CH"] += 1
        last.then_inc(S["CH"], 1)
        e.wait_ge(S["CH"], C["CH"])
        for c in range(NCH):
            last = e.tensor_scalar(out=T["GMEANB"][:, c:c + 1],
                                   in0=MISC[:, M_SCR + c:M_SCR + c + 1],
                                   scalar1=1.0 / 4096.0, scalar2=None, op0=OP.mult)
        inc(last, "VE")
    else:
        C["CH"] += 1
    note("VE")
    L["gmean"] = C["VE"]

    for b in (4, 5, 6, 7):
        pe_claim(b)
    if mine("tensor"):
        wait("VE", L["gmean"])
        wait("WD", L["gcw"])
        last = None
        for mc in range(4):
            for kc in range(8):
                last = e.matmul(bank_ap(4 + mc, 128, 1),
                                T["GC1"][:, kc * 512 + mc * 128: kc * 512 + (mc + 1) * 128],
                                T["GMEANB"][:, kc:kc + 1],
                                start=(kc == 0), stop=(kc == 7))
        inc(last, "PE")
    note("PE")
    L["g1mm"] = C["PE"]

    if mine("scalar"):
        wait("PE", L["g1mm"])
        wait("B")
        last = None
        for mc in range(4):
            last = e.activation(out=T["G1B"][:, mc:mc + 1],
                                in_=bank_ap(4 + mc, 128, 1), func=AF.Gelu,
                                bias=T["GB1"][:, mc:mc + 1])
        inc(last, "SC")
    note("SC")
    L["g1evac"] = C["SC"]
    for b in (4, 5, 6, 7):
        bank_read_done(b, "SC")

    pe_claim(0)
    pe_claim(1)
    if mine("tensor"):
        wait("SC", L["g1evac"])
        last = None
        for mc in range(2):
            for kc in range(4):
                last = e.matmul(bank_ap(mc, 128, 1),
                                T["GC2"][:, kc * 256 + mc * 128: kc * 256 + (mc + 1) * 128],
                                T["G1B"][:, kc:kc + 1],
                                start=(kc == 0), stop=(kc == 3))
        inc(last, "PE")
    note("PE")
    L["g2mm"] = C["PE"]

    if mine("scalar"):
        wait("PE", L["g2mm"])
        last = None
        for mc in range(2):
            last = e.activation(out=T["G2B"][:, mc:mc + 1],
                                in_=bank_ap(mc, 128, 1), func=AF.Gelu,
                                bias=T["GB2"][:, mc:mc + 1])
        inc(last, "SC")
    note("SC")
    L["g2evac"] = C["SC"]
    bank_read_done(0, "SC")
    bank_read_done(1, "SC")

    pe_claim(2)
    if mine("tensor"):
        wait("SC", L["g2evac"])
        last = None
        for kc in range(2):
            last = e.matmul(bank_ap(2, 1, 1), T["GC3"][:, kc:kc + 1],
                            T["G2B"][:, kc:kc + 1], start=(kc == 0), stop=(kc == 1))
        inc(last, "PE")
    note("PE")
    L["g3mm"] = C["PE"]

    if mine("scalar"):
        wait("PE", L["g3mm"])
        last = e.activation(out=scr(S_GF, 1), in_=bank_ap(2, 1, 1),
                            func=AF.Sigmoid, bias=misc(M_GB3))
        inc(last, "SC")
    note("SC")
    L["gf"] = C["SC"]
    bank_read_done(2, "SC")

    if mine("vector"):
        wait("SC", L["gf"])
        def _step(inst):
            C["CH"] += 1
            inst.then_inc(S["CH"], 1)
            e.wait_ge(S["CH"], C["CH"])
        _step(e.tensor_scalar(out=scr(S_K2, 1), in0=scr(S_AR, 1), scalar1=0.6,
                              scalar2=None, op0=OP.mult))
        _step(e.tensor_scalar(out=scr(S_T11, 1), in0=scr(S_GF, 1),
                              scalar1=scr(S_AR, 1), scalar2=None, op0=OP.mult))
        _step(e.tensor_scalar(out=scr(S_T11, 1), in0=scr(S_T11, 1), scalar1=1.4,
                              scalar2=0.1, op0=OP.mult, op1=OP.add))
        _step(e.tensor_tensor(out=scr(S_K1, 1), in0=scr(S_T11, 1),
                              in1=scr(S_AR, 1), op=OP.subtract))
        inc(e.engine_nop(), "VE")
    else:
        C["CH"] += 4
    note("VE")
    L["kconst"] = C["VE"]

    if mine("gpsimd"):
        wait("PE", L["g3mm"])
        for c in range(8):
            inc(e.dma_start(out=T["WOT"][:, c * 1024:(c + 1) * 1024],
                            in_=T["wot_d"][c * 128:(c + 1) * 128, :]), "WD", 16)
    note("WD", 16 * 8)
    L["wvw"] = C["WD"]

    for mc in range(8):
        b = mc % 4
        pe_claim(b)
        if mine("tensor"):
            if mc == 0:
                wait("SC", L[("vevac", 7)])
                wait("WD", L["wvw"])
            last = None
            for kc in range(8):
                last = e.matmul(bank_ap(b, 128, 1),
                                T["WOT"][:, kc * 1024 + mc * 128: kc * 1024 + (mc + 1) * 128],
                                T["VB"][:, kc:kc + 1],
                                start=(kc == 0), stop=(kc == 7))
            inc(last, "PE")
        note("PE")
        L[("rmm", mc)] = C["PE"]
        if mine("scalar"):
            wait("PE", L[("rmm", mc)])
            last = e.activation(out=T["RSB"][:, mc:mc + 1],
                                in_=bank_ap(b, 128, 1), func=AF.Copy)
            inc(last, "SC")
        note("SC")
        L[("revac", mc)] = C["SC"]
        bank_read_done(b, "SC")

    if mine("gpsimd"):
        wait("PE", L[("rmm", 7)])
        for c in range(8):
            inc(e.dma_start(out=T["W1"][:, c * 1024:(c + 1) * 1024],
                            in_=T["w1_d"][c * 128:(c + 1) * 128, :]), "W2", 16)
        for c in range(8):
            inc(e.dma_start(out=T["W2"][:, c * 512:(c + 1) * 512],
                            in_=T["w2_d"][c * 128:(c + 1) * 128, :]), "W2", 16)
        for c in range(4):
            inc(e.dma_start(out=T["W3"][:, c * 256:(c + 1) * 256],
                            in_=T["w3_d"][c * 128:(c + 1) * 128, :]), "W2", 16)
        for c in range(2):
            inc(e.dma_start(out=T["W4"][:, c:c + 1],
                            in_=T["w4_d"][c * 128:(c + 1) * 128, :]), "W2", 16)
    note("W2", 16 * 22)
    L["cnw"] = C["W2"]

    n_cycles = {"cycle1": 1, "cycle2": 2, "cycle3": 3, "full": 3}[stage]

    def emit_apply(pcyc, ptj):
        if mine("vector"):
            wait("SC", L[("cpevac", pcyc, ptj)])
            last = None
            for q in range(4):
                e.tensor_tensor(
                    out=T["TMP"][:, 0:TT],
                    in0=OUT[:, TWIST_SRC[q] * D + ptj * TT:
                            TWIST_SRC[q] * D + (ptj + 1) * TT],
                    in1=T["CPS"][:, (ptj % 4) * TT:(ptj % 4 + 1) * TT],
                    op=OP.mult)
                last = e.tensor_tensor(
                    out=T["TMP"][:, TT:2 * TT],
                    in0=OUT[:, q * D + ptj * TT: q * D + (ptj + 1) * TT],
                    in1=T["CPS"][:, (ptj % 4) * TT:(ptj % 4 + 1) * TT],
                    op=OP.mult)
                if q == 3:
                    inc(last, "VE")
                e.tensor_tensor(
                    out=OUT[:, q * D + ptj * TT: q * D + (ptj + 1) * TT],
                    in0=OUT[:, q * D + ptj * TT: q * D + (ptj + 1) * TT],
                    in1=T["TMP"][:, 0:TT], op=TWIST_OP[q])
                last = e.tensor_tensor(
                    out=OUT[:, (q + 4) * D + ptj * TT:
                            (q + 4) * D + (ptj + 1) * TT],
                    in0=OUT[:, (q + 4) * D + ptj * TT:
                            (q + 4) * D + (ptj + 1) * TT],
                    in1=T["TMP"][:, TT:2 * TT], op=TWIST_OP[q + 4])
            inc(last, "VE")
        note("VE", 2)
        L[("applym", pcyc, ptj)] = C["VE"] - 1
        L[("apply", pcyc, ptj)] = C["VE"]

    def emit_cpevac(pcyc, ptj, i, pp):
        b = BC_BANK[i]
        if mine("scalar"):
            wait("PE", L[("bc", pcyc, ptj)])
            if pp is not None:
                wait("VE", L[("applym", pp[0], pp[1] + i)])
            last = e.activation(out=T["CPS"][:, i * TT:(i + 1) * TT],
                                in_=bank_ap(b), func=AF.Copy)
            inc(last, "SC")
        note("SC")
        L[("cpevac", pcyc, ptj)] = C["SC"]
        bank_read_done(b, "SC")

    def half_before(cyc, half):
        return (cyc, 0) if half == 1 else ((cyc - 1, 1) if cyc > 0 else None)

    for cyc in range(n_cycles):
        for half in range(2):
            tiles = [half * 4 + i for i in range(4)]
            prev = half_before(cyc, half)

            if prev is not None:
                pp = half_before(*prev)
                for i in range(4):
                    emit_cpevac(prev[0], prev[1] * 4 + i, i,
                                (pp[0], pp[1] * 4) if pp is not None else None)

            for i, tj in enumerate(tiles):
                if mine("vector"):
                    pg = tj - 1 if tj > 0 else (NT - 1 if cyc > 0 else None)
                    if pg is not None:
                        pc = cyc if tj > 0 else cyc - 1
                        wait("PE", L[("h1mm", pc, pg, 7)])
                    last = None
                    for c in range(NCH):
                        last = e.tensor_copy(
                            out=T["OUTB"][:, c * TT:(c + 1) * TT],
                            in_=OUT[:, c * D + tj * TT: c * D + (tj + 1) * TT])
                    inc(last, "VE")
                note("VE")
                L[("cast", cyc, tj)] = C["VE"]

                if prev is not None:
                    emit_apply(prev[0], prev[1] * 4 + i)
                for mc in range(8):
                    pe_claim(mc)
                    if mine("tensor"):
                        if mc == 0:
                            wait("VE", L[("cast", cyc, tj)])
                            if cyc == 0 and tj == 0:
                                wait("W2", L["cnw"])
                        last = None
                        for kc in range(8):
                            last = e.matmul(
                                bank_ap(mc),
                                T["W1"][:, kc * 1024 + mc * 128: kc * 1024 + (mc + 1) * 128],
                                T["OUTB"][:, kc * TT:(kc + 1) * TT],
                                start=(kc == 0), stop=(kc == 7))
                        inc(last, "PE")
                    note("PE")
                    L[("h1mm", cyc, tj, mc)] = C["PE"]

                for mc in range(8):
                    if mine("scalar"):
                        wait("PE", L[("h1mm", cyc, tj, mc)])
                        last = e.activation(out=T["H1B"][:, mc * TT:(mc + 1) * TT],
                                            in_=bank_ap(mc), func=AF.Gelu,
                                            bias=T["B1"][:, mc:mc + 1])
                        inc(last, "SC")
                    note("SC")
                    L[("h1evac", cyc, tj, mc)] = C["SC"]
                    bank_read_done(mc, "SC")

                for mc in range(4):
                    pe_claim(mc)
                    if mine("tensor"):
                        if mc == 0:
                            wait("SC", L[("h1evac", cyc, tj, 7)])
                        last = None
                        for kc in range(8):
                            last = e.matmul(
                                bank_ap(mc),
                                T["W2"][:, kc * 512 + mc * 128: kc * 512 + (mc + 1) * 128],
                                T["H1B"][:, kc * TT:(kc + 1) * TT],
                                start=(kc == 0), stop=(kc == 7))
                        inc(last, "PE")
                    note("PE")
                    L[("h2mm", cyc, tj, mc)] = C["PE"]

                for mc in range(4):
                    if mine("scalar"):
                        wait("PE", L[("h2mm", cyc, tj, mc)])
                        last = e.activation(out=T["H2B"][:, mc * TT:(mc + 1) * TT],
                                            in_=bank_ap(mc), func=AF.Gelu,
                                            bias=T["B2"][:, mc:mc + 1])
                        inc(last, "SC")
                    note("SC")
                    L[("h2evac", cyc, tj, mc)] = C["SC"]
                    bank_read_done(mc, "SC")

                for mc in range(2):
                    pe_claim(4 + mc)
                    if mine("tensor"):
                        if mc == 0:
                            wait("SC", L[("h2evac", cyc, tj, 3)])
                        last = None
                        for kc in range(4):
                            last = e.matmul(
                                bank_ap(4 + mc),
                                T["W3"][:, kc * 256 + mc * 128: kc * 256 + (mc + 1) * 128],
                                T["H2B"][:, kc * TT:(kc + 1) * TT],
                                start=(kc == 0), stop=(kc == 3))
                        inc(last, "PE")
                    note("PE")
                    L[("h3mm", cyc, tj, mc)] = C["PE"]

                for mc in range(2):
                    if mine("scalar"):
                        wait("PE", L[("h3mm", cyc, tj, mc)])
                        last = e.activation(out=T["H3B"][:, mc * TT:(mc + 1) * TT],
                                            in_=bank_ap(4 + mc), func=AF.Gelu,
                                            bias=T["B3"][:, mc:mc + 1])
                        inc(last, "SC")
                    note("SC")
                    L[("h3evac", cyc, tj, mc)] = C["SC"]
                    bank_read_done(4 + mc, "SC")

                hb = 6 + (i % 2)
                pe_claim(hb)
                if mine("tensor"):
                    wait("SC", L[("h3evac", cyc, tj, 1)])
                    last = None
                    for kc in range(2):
                        last = e.matmul(bank_ap(hb, 1), T["W4"][:, kc:kc + 1],
                                        T["H3B"][:, kc * TT:(kc + 1) * TT],
                                        start=(kc == 0), stop=(kc == 1))
                    inc(last, "PE")
                note("PE")
                L[("h4", cyc, tj)] = C["PE"]

                if mine("scalar"):
                    wait("PE", L[("h4", cyc, tj)])
                    last = e.activation(out=scr(H4R + i * 512),
                                        in_=bank_ap(hb, 1), func=AF.Copy)
                    inc(last, "SC")
                note("SC")
                L[("h4evac", cyc, tj)] = C["SC"]
                bank_read_done(hb, "SC")

            if mine("scalar"):
                if prev is not None:
                    wait("VE", L[("cprow",) + prev])
                last = e.activation(out=scr(SIGR, 2048), in_=scr(H4R, 2048),
                                    func=AF.Sigmoid, bias=misc(M_B4))
                inc(last, "SC")
            note("SC")
            L[("sig", cyc, half)] = C["SC"]

            if mine("vector"):
                wait("SC", L[("sig", cyc, half)])
                if cyc == 0 and half == 0:
                    wait("VE", L["kconst"])
                last = e.tensor_scalar(out=scr(H4R, 2048), in0=scr(SIGR, 2048),
                                       scalar1=scr(S_K2, 1), scalar2=scr(S_K1, 1),
                                       op0=OP.mult, op1=OP.add)
                inc(last, "VE")
            note("VE")
            L[("cprow", cyc, half)] = C["VE"]

            for i, tj in enumerate(tiles):
                b = BC_BANK[i]
                pe_claim(b)
                if mine("tensor"):
                    if i == 0:
                        wait("VE", L[("cprow", cyc, half)])
                    last = e.matmul(bank_ap(b), T["ONESB"][64:65, :],
                                    scr(H4R + i * 512), start=True, stop=True)
                    inc(last, "PE")
                note("PE")
                L[("bc", cyc, tj)] = C["PE"]

    lastc = n_cycles - 1
    for i in range(4):
        emit_cpevac(lastc, 4 + i, i, (lastc, 0))
    for i in range(4):
        emit_apply(lastc, 4 + i)

    if stage in ("cycle1", "cycle2", "cycle3"):
        _dump_out_and_finish(me, e, T, S, C, L,
                             after=("apply", lastc, NT - 1))
        return

    for j in range(NT):
        xt = T["XT0"] if j % 2 == 0 else T["XT1"]
        bnm = 2 if j % 2 == 0 else 6
        brs = bnm + 1
        if mine("gpsimd"):
            if j <= 1:
                wait("PE", L[("bc", 2, NT - 1)])
                wait("SC", L[("h4evac", 2, NT - 1)])
            else:
                wait("ST", 16 * (j - 1))
            xsrc = T["x_d"].rearrange("(c p) t -> p c t", p=128)
            inc(e.dma_start(out=xt[:, :].rearrange("p (c t) -> p c t", c=NCH),
                            in_=xsrc[:, :, j * TT:(j + 1) * TT]),
                "XTA" if j % 2 == 0 else "XTB", 16)
        xsem = "XTA" if j % 2 == 0 else "XTB"
        note(xsem, 16)
        L[("xt", j)] = (xsem, C[xsem])

        pe_claim(brs)
        if mine("tensor"):
            wait("VE", L[("apply", 2, j)])
            last = e.matmul(bank_ap(brs), T["ONESB"][0:1, :],
                            ROWS[0:1, j * TT:(j + 1) * TT], start=True, stop=True)
            inc(last, "PE")
        note("PE")
        L[("p3bc", j)] = C["PE"]

        pe_claim(bnm)
        if mine("tensor"):
            last = e.matmul(bank_ap(bnm), T["ONESB"][32:33, :],
                            ROWS[32:33, j * TT:(j + 1) * TT], start=True, stop=True)
            inc(last, "PE")
        note("PE")
        L[("p3w", j)] = C["PE"]

        for h in range(2):
            ch = range(4 * h, 4 * h + 4)
            if mine("scalar"):
                if j == 0 and h == 0:
                    wait("VE", L[("apply", 2, NT - 1)])
                elif j > 0:
                    wait("VE", L[("p3half", j - 1, h)])
                last = None
                for c in ch:
                    last = e.activation(out=T["GT"][:, c * TT:(c + 1) * TT],
                                        in_=OUT[:, c * D + j * TT: c * D + (j + 1) * TT],
                                        func=AF.Silu)
                inc(last, "SC")
            note("SC")
            L[("p3silu", j, h)] = C["SC"]

            if mine("vector"):
                if h == 0:
                    wait(*L[("xt", j)])
                    wait("PE", L[("p3w", j)])
                wait("SC", L[("p3silu", j, h)])
                last = None
                W = 384 if j == NT - 1 else TT
                for c in ch:
                    if j == NT - 1:
                        e.tensor_scalar(
                            out=T["TMP"][:, TT + 384:TT + 512],
                            in0=T["GT"][:, c * TT + 384: c * TT + 512],
                            scalar1=T["RSB"][:, c:c + 1], scalar2=None, op0=OP.mult)
                    e.tensor_tensor(out=T["GT"][:, c * TT:c * TT + W],
                                    in0=T["GT"][:, c * TT:c * TT + W],
                                    in1=bank_ap(brs, 128, W), op=OP.mult)
                    last = e.tensor_tensor(out=T["TMP"][:, 0:W],
                                           in0=xt[:, c * TT:c * TT + W],
                                           in1=bank_ap(bnm, 128, W), op=OP.add)
                    if c == NCH - 1:
                        inc(last, "VE")
                    e.tensor_tensor(out=T["TMP"][:, 0:W],
                                    in0=T["TMP"][:, 0:W],
                                    in1=T["GT"][:, c * TT:c * TT + W], op=OP.mult)
                    last = e.tensor_tensor(out=xt[:, c * TT:c * TT + W],
                                           in0=xt[:, c * TT:c * TT + W],
                                           in1=T["TMP"][:, 0:W], op=OP.add)
                    if j == NT - 1:
                        last = e.tensor_tensor(
                            out=xt[:, c * TT + 384: c * TT + 512],
                            in0=xt[:, c * TT + 384: c * TT + 512],
                            in1=T["TMP"][:, TT + 384:TT + 512], op=OP.add)
                inc(last, "VE")
            note("VE", 2 if h == 1 else 1)
            L[("p3half", j, h)] = C["VE"]
        L[("p3u", j)] = C["VE"] - 1
        L[("p3dve", j)] = C["VE"]
        bank_read_done(bnm, "VE")
        bank_read_done(brs, "VE")

        if mine("sync"):
            wait("VE", L[("p3dve", j)])
            ydst = T["y_d"].rearrange("(c p) t -> p c t", p=128)
            inc(e.dma_start(out=ydst[:, :, j * TT:(j + 1) * TT],
                            in_=xt[:, :].rearrange("p (c t) -> p c t", c=NCH)),
                "ST", 16)
        note("ST", 16)

    if mine("sync"):
        wait("ST")


def _dump_out_and_finish(me, e, T, S, C, L, after=None):
    C["ST"] += 16 * NCH
    if me == "sync":
        if after is None:
            e.wait_ge(S["VE"], L[("p1norm", NT - 1)])
        else:
            e.wait_ge(S["VE"], L[after])
        for c in range(NCH):
            e.dma_start(out=T["y_d"][c * 128:(c + 1) * 128, :],
                        in_=T["OUT"][:, c * D:(c + 1) * D]).then_inc(S["ST"], 16)
        e.wait_ge(S["ST"], C["ST"])


def build_program(stage="full"):
    nc = bass.Bass("TRN2", target_bir_lowering=False, debug=False)
    T = {}
    T["x_d"] = nc.dram_tensor("x", [1024, 4096], F32, kind="ExternalInput").ap()
    for nm, shape in (("w1", [1024, 1024]), ("w2", [1024, 512]),
                      ("w3", [512, 256]), ("w4", [256, 1]),
                      ("g1", [1024, 512]), ("g2", [512, 256]), ("g3", [256, 1]),
                      ("wvt", [1024, 1024]), ("wot", [1024, 1024]),
                      ("b1r", [128, 8]), ("b2r", [128, 4]), ("b3r", [128, 2]),
                      ("gb1r", [128, 4]), ("gb2r", [128, 2]),
                      ("b4r", [1, 1]), ("gb3r", [1, 1]), ("arr", [1, 1]),
                      ("eye", [128, 128])):
        T[nm + "_d"] = nc.dram_tensor(nm, shape, F32, kind="ExternalInput").ap()
    T["y_d"] = nc.dram_tensor("y", [1024, 4096], F32, kind="ExternalOutput").ap()

    T.update(_alloc_tensors(nc))

    from contextlib import ExitStack
    with ExitStack() as stack:
        S = {}
        for nm in ("X", "B", "WD", "W2", "PE", "SC", "VE", "ST", "XT", "CH",
                   "X0", "X1", "X2", "X3", "X4", "X5", "X6", "X7",
                   "XTA", "XTB"):
            S[nm] = stack.enter_context(nc.semaphore(nm))
        with nc.Block() as block:
            block.sync(lambda e: _emit("sync", e, nc, T, S, stage))
            block.gpsimd(lambda e: _emit("gpsimd", e, nc, T, S, stage))
            block.tensor(lambda e: _emit("tensor", e, nc, T, S, stage))
            block.scalar(lambda e: _emit("scalar", e, nc, T, S, stage))
            block.vector(lambda e: _emit("vector", e, nc, T, S, stage))
    return nc


def make_in_maps(inputs):
    x = inputs["L0_attn_inputs"]
    c = np.ascontiguousarray
    shared = {
        "w1": c(inputs["L0_ffn_cn_w1"]), "w2": c(inputs["L0_ffn_cn_w2"]),
        "w3": c(inputs["L0_ffn_cn_w3"]), "w4": c(inputs["L0_ffn_cn_w4"]),
        "g1": c(inputs["L0_ffn_gc_w1"]), "g2": c(inputs["L0_ffn_gc_w2"]),
        "g3": c(inputs["L0_ffn_gc_w3"]),
        "wvt": c(inputs["L0_attn_Wv"].T), "wot": c(inputs["L0_attn_Wo"].T),
        "b1r": c(inputs["L0_ffn_cn_b1"].reshape(8, 128).T),
        "b2r": c(inputs["L0_ffn_cn_b2"].reshape(4, 128).T),
        "b3r": c(inputs["L0_ffn_cn_b3"].reshape(2, 128).T),
        "gb1r": c(inputs["L0_ffn_gc_b1"].reshape(4, 128).T),
        "gb2r": c(inputs["L0_ffn_gc_b2"].reshape(2, 128).T),
        "b4r": c(inputs["L0_ffn_cn_b4"].reshape(1, 1)),
        "gb3r": c(inputs["L0_ffn_gc_b3"].reshape(1, 1)),
        "eye": np.eye(128, dtype=np.float32),
        "arr": c(inputs["L0_ffn_adaptive_range"].reshape(1, 1)),
    }
    in_maps = []
    for b in range(8):
        m = dict(shared)
        m["x"] = c(x[b].T)
        in_maps.append(m)
    return in_maps


def kernel(**inputs):
    nc = build_program("full")
    in_maps = make_in_maps(inputs)
    res = run_bass_kernel_spmd(nc, in_maps, core_ids=list(range(8)))
    out = np.empty((8, 4096, 1024), dtype=np.float32)
    for b in range(8):
        out[b] = res.results[b]["y"].T
    return out
